# version 01
# baseline (speedup 1.0000x reference)
import numpy as np
from contextlib import ExitStack
import concourse.bass as bass
import concourse.mybir as mybir
from concourse.bass_utils import run_bass_kernel_spmd

F32 = mybir.dt.float32
BF16 = mybir.dt.bfloat16
AF = mybir.ActivationFunctionType
ALU = mybir.AluOpType
AX = mybir.AxisListType

L = 4096
D = 1024
TT = 512
NT = L // TT
DC = D // 128
NEG = -1.0e30

DEV_UPTO = None
DEV_DUMP = ()
DEV_STAGE = 99

DEV_TILES = None
DEV_DBGPAIR = False
DEV_MOE_PASSES = 9



def _rotary_maps():
    DIM_HEAD, WINDOW, n = 64, 512, L
    half_idx = np.arange(0, DIM_HEAD, 2, dtype=np.float32)
    inv_freq = 1.0 / 10000.0 ** (half_idx.astype(np.float64) / DIM_HEAD)
    t = np.arange(n, dtype=np.float64)
    freqs = t[:, None] * inv_freq[None, :]
    freqs = np.concatenate([freqs, freqs], axis=-1)
    base = (half_idx.astype(np.float64) + 0.4 * DIM_HEAD) / (1.4 * DIM_HEAD)
    power = (t - n // 2) / float(WINDOW // 2)
    scale = base[None, :] ** power[:, None]
    scale = np.concatenate([scale, scale], axis=-1)
    cos, sin = np.cos(freqs), np.sin(freqs)
    CQ = (cos * scale).T.astype(np.float32)
    SQ = (sin * scale).T.astype(np.float32)
    CK = (cos / scale).T.astype(np.float32)
    SK = (sin / scale).T.astype(np.float32)
    return CQ, SQ, CK, SK


def _host_consts():
    c = {}
    c["c_ident"] = np.eye(128, dtype=np.float32)
    r = np.arange(128).reshape(128, 1)
    cc = np.arange(128).reshape(1, 128)
    c["c_mask_geq"] = np.where(cc >= r, 0.0, NEG).astype(np.float32)
    c["c_mask_leq"] = np.where(cc <= r, 0.0, NEG).astype(np.float32)
    c["c_pos"] = np.arange(L, dtype=np.float32).reshape(1, L)
    CQ, SQ, CK, SK = _rotary_maps()
    c["c_cq"], c["c_sq"], c["c_ck"], c["c_sk"] = CQ, SQ, CK, SK
    R = np.zeros((128, 128), np.float32)
    for blk in range(2):
        o = 64 * blk
        for dd in range(32):
            R[o + dd + 32, o + dd] = -1.0
            R[o + dd, o + dd + 32] = 1.0
    c["c_r128"] = R
    return c


def prep_shared(inputs):
    f = lambda a: np.ascontiguousarray(np.asarray(a, dtype=np.float32))
    g = {}
    for n in ("Wq", "Wk", "Wv", "Wo", "Wb", "Wfd", "Wsd", "Wtf1"):
        g["w_l0a_" + n] = f(inputs["L0_attn_" + n])
    g["w_l0a_bfd"] = f(inputs["L0_attn_bfd"]).reshape(1, 16)
    g["w_l0a_bsd"] = f(inputs["L0_attn_bsd"]).reshape(1, 16)
    g["w_l0a_btf1"] = f(inputs["L0_attn_btf1"]).reshape(1, 32)
    g["w_l0a_wtf2"] = f(inputs["L0_attn_Wtf2"]).reshape(1, 32)
    g["w_l0a_btf2"] = f(inputs["L0_attn_btf2"]).reshape(1, 1)
    w1 = f(inputs["L0_ffn_w1"])
    b1 = f(inputs["L0_ffn_b1"])
    w2 = f(inputs["L0_ffn_w2"])
    HID = 2730
    HP = 2816
    w1p = np.zeros((2, D, 2 * HP), np.float32)
    b1p = np.zeros((2, 1, 2 * HP), np.float32)
    w1p[:, :, :HID] = w1[:, :, :HID]
    w1p[:, :, HP:HP + HID] = w1[:, :, HID:]
    b1p[:, 0, :HID] = b1[:, :HID]
    b1p[:, 0, HP:HP + HID] = b1[:, HID:]
    w2p = np.zeros((2, HP, D), np.float32)
    w2p[:, :HID, :] = w2
    perm = np.concatenate([np.arange(32, 64), np.arange(0, 32)])
    for ll in range(2):
        g[f"w_ff{ll}_wqkv"] = f(inputs["L0_ffn_wqkv"][ll])
        g[f"w_ff{ll}_qs"] = f(inputs["L0_ffn_q_scale"][ll]).reshape(1, 64)
        g[f"w_ff{ll}_qsp"] = f(inputs["L0_ffn_q_scale"][ll])[perm].reshape(1, 64)
        g[f"w_ff{ll}_ks"] = f(inputs["L0_ffn_k_scale"][ll]).reshape(1, 64)
        g[f"w_ff{ll}_ksp"] = f(inputs["L0_ffn_k_scale"][ll])[perm].reshape(1, 64)
        g[f"w_ff{ll}_wo"] = f(inputs["L0_ffn_wo"][ll])
        g[f"w_ff{ll}_lng"] = f(inputs["L0_ffn_ln_g"][ll]).reshape(8, 128)
        g[f"w_ff{ll}_lnb"] = f(inputs["L0_ffn_ln_b"][ll]).reshape(8, 128)
        g[f"w_ff{ll}_w1"] = np.ascontiguousarray(w1p[ll])
        g[f"w_ff{ll}_b1"] = np.ascontiguousarray(b1p[ll])
        g[f"w_ff{ll}_w2"] = np.ascontiguousarray(w2p[ll])
        g[f"w_ff{ll}_b2"] = f(inputs["L0_ffn_b2"][ll]).reshape(1, D)
    g["w_l1a_wqkv"] = f(inputs["L1_attn_W_qkv"])
    g["w_l1a_bqkv"] = f(inputs["L1_attn_b_qkv"]).reshape(1, 3072)
    g["w_l1a_wout"] = f(inputs["L1_attn_W_out"])
    g["w_l1a_bout"] = f(inputs["L1_attn_b_out"]).reshape(1, D)
    g["w_moe_sw1"] = np.ascontiguousarray(f(inputs["L1_ffn_shared_w1"]).T)
    g["w_moe_sb1"] = f(inputs["L1_ffn_shared_b1"]).reshape(1, 4096)
    g["w_moe_sw2"] = np.ascontiguousarray(f(inputs["L1_ffn_shared_w2"]).T)
    g["w_moe_sb2"] = f(inputs["L1_ffn_shared_b2"]).reshape(1, D)
    g["w_moe_rw"] = np.ascontiguousarray(f(inputs["L1_ffn_router_w"]).T)
    g["w_moe_rb"] = f(inputs["L1_ffn_router_b"]).reshape(1, 8)
    g["w_moe_ew1"] = np.ascontiguousarray(f(inputs["L1_ffn_spec_w1"]).transpose(0, 2, 1))
    g["w_moe_eb1"] = f(inputs["L1_ffn_spec_b1"])
    g["w_moe_ew2"] = np.ascontiguousarray(f(inputs["L1_ffn_spec_w2"]).transpose(0, 2, 1))
    g["w_moe_eb2"] = f(inputs["L1_ffn_spec_b2"])
    g.update(_host_consts())
    return g



class Sem:
    __slots__ = ("h", "n")

    def __init__(self, h):
        self.h = h
        self.n = 0

    def inc(self, inst, k=1):
        inst.then_inc(self.h, k)
        self.n += k

    def wait(self, eng, val=None):
        eng.wait_ge(self.h, self.n if val is None else val)


class PPool:

    def __init__(self, K, name, shape, n=2, dtype=F32):
        self.K = K
        self.n = n
        self.tiles = [K.es.enter_context(K.nc.psum_tensor(f"{name}{i}", list(shape), dtype))
                      for i in range(n)]
        self.sem = Sem(K.es.enter_context(K.nc.semaphore(f"sem_{name}")))
        self.uses = 0
        self.dones = 0

    def take(self):
        b = self.uses % self.n
        need = self.uses - self.n + 1
        self.uses += 1
        if need > 0:
            self.K.pe.wait_ge(self.sem.h, need)
        return self.tiles[b]

    def done(self, inst, k=1):
        self.sem.inc(inst, k)
        self.dones += k


class Kctx:
    def __init__(self, nc, es):
        self.nc = nc
        self.es = es
        self.pe = nc.tensor
        self.dve = nc.vector
        self.act = nc.scalar
        self.pool = nc.gpsimd
        self.sp = nc.sync
        self.engines = {"pe": self.pe, "dve": self.dve, "act": self.act,
                        "pool": self.pool, "sp": self.sp}
        self.sem = {}
        for name in ("ld", "ldg", "st", "stg", "pe", "dve", "act", "pool", "sp"):
            self.sem[name] = Sem(es.enter_context(nc.semaphore("sem_" + name)))
        self.inp = {}
        self._names = {}
        self.scr = {}
        self.out = {}
        self.consts = {}
        self.phase_es = es

    def _uniq(self, name):
        n = self._names.get(name, 0)
        self._names[name] = n + 1
        return name if n == 0 else f"{name}__{n}"

    def sb(self, name, shape, dt, phase=True):
        es = self.phase_es if phase else self.es
        return es.enter_context(self.nc.sbuf_tensor(self._uniq(name), list(shape), dt))

    def pp(self, name, shape, n=2, dtype=F32):
        K2 = self
        class _tmp:
            pass
        name = self._uniq(name)
        p = PPool.__new__(PPool)
        p.K = K2
        p.n = n
        p.tiles = [self.phase_es.enter_context(self.nc.psum_tensor(f"{name}{i}", list(shape), dtype))
                   for i in range(n)]
        p.sem = Sem(self.phase_es.enter_context(self.nc.semaphore(f"sem_{name}")))
        p.uses = 0
        p.dones = 0
        return p

    def load(self, dst, src, cast=False):
        if cast:
            self.sem["ldg"].inc(self.pool.dma_start(out=dst, in_=src), 16)
        else:
            self.sem["ld"].inc(self.sp.dma_start(out=dst, in_=src), 16)

    def store(self, dst, src, eng=None):
        eng = eng or self.pool
        nm = "st" if eng is self.sp else "stg"
        self.sem[nm].inc(eng.dma_start(out=dst, in_=src), 16)

    def wait_stores(self, eng):
        self.sem["st"].wait(eng)
        self.sem["stg"].wait(eng)

    def wait_loads(self, eng):
        self.sem["ld"].wait(eng)
        self.sem["ldg"].wait(eng)

    def tile_ap(self, dram, t0, w, chunks=DC, csz=128):
        return dram[:, t0:t0 + w].rearrange("(c p) t -> p c t", p=csz)

    def bump(self, eng_name, sem, k=1):
        eng = self.engines[eng_name]
        self.sem[eng_name].wait(eng)
        sem.inc(eng.nop(), k)

    def barrier(self):
        for nm2, eng in self.engines.items():
            for nm in ("pe", "dve", "act", "pool", "sp", "ld", "ldg", "st", "stg"):
                if nm != nm2:
                    self.sem[nm].wait(eng)


def declare_io(nc, K, shapes_only=False):
    def din(name, shape, dt=F32):
        K.inp[name] = nc.dram_tensor(name, list(shape), dt, kind="ExternalInput").ap()

    din("x_fm", [D, L])
    din("w_l0a_Wq", [D, D]); din("w_l0a_Wk", [D, D]); din("w_l0a_Wv", [D, D])
    din("w_l0a_Wo", [D, D])
    din("w_l0a_Wb", [D, 16]); din("w_l0a_Wfd", [D, 16]); din("w_l0a_Wsd", [D, 16])
    din("w_l0a_bfd", [1, 16]); din("w_l0a_bsd", [1, 16])
    din("w_l0a_Wtf1", [64, 32]); din("w_l0a_btf1", [1, 32])
    din("w_l0a_wtf2", [1, 32]); din("w_l0a_btf2", [1, 1])
    for ll in range(2):
        din(f"w_ff{ll}_wqkv", [D, 1536])
        din(f"w_ff{ll}_qs", [1, 64]); din(f"w_ff{ll}_qsp", [1, 64])
        din(f"w_ff{ll}_ks", [1, 64]); din(f"w_ff{ll}_ksp", [1, 64])
        din(f"w_ff{ll}_wo", [512, D])
        din(f"w_ff{ll}_lng", [8, 128]); din(f"w_ff{ll}_lnb", [8, 128])
        din(f"w_ff{ll}_w1", [D, 5632]); din(f"w_ff{ll}_b1", [1, 5632])
        din(f"w_ff{ll}_w2", [2816, D]); din(f"w_ff{ll}_b2", [1, D])
    din("w_l1a_wqkv", [D, 3072]); din("w_l1a_bqkv", [1, 3072])
    din("w_l1a_wout", [D, D]); din("w_l1a_bout", [1, D])
    din("w_moe_sw1", [D, 4096]); din("w_moe_sb1", [1, 4096])
    din("w_moe_sw2", [4096, D]); din("w_moe_sb2", [1, D])
    din("w_moe_rw", [D, 8]); din("w_moe_rb", [1, 8])
    din("w_moe_ew1", [8, D, 4096]); din("w_moe_eb1", [8, 4096])
    din("w_moe_ew2", [8, 4096, D]); din("w_moe_eb2", [8, 1024])
    din("c_ident", [128, 128]); din("c_mask_geq", [128, 128]); din("c_mask_leq", [128, 128])
    din("c_pos", [1, L])
    din("c_cq", [64, L]); din("c_sq", [64, L]); din("c_ck", [64, L]); din("c_sk", [64, L])
    din("c_r128", [128, 128])
    def scr(name, shape, dt):
        K.scr[name] = nc.dram_tensor(name, list(shape), dt, kind="Internal").ap()

    scr("s_x", [D, L], F32)
    scr("s_x2", [D, L], F32)
    scr("s_a", [D, L], F32)
    scr("s_xnb", [D, L], BF16)
    scr("s_xn", [D, L], F32)
    scr("s_q", [D, L], BF16)
    scr("s_k", [D, L], BF16)
    scr("s_v", [D, L], BF16)
    scr("s_o", [D, L], BF16)
    scr("s_y", [D, L], F32)
    scr("s_y2", [D, L], F32)
    scr("s_qkv", [1536, L], BF16)
    scr("s_qr", [512, L], BF16)
    scr("s_kr", [512, L], BF16)
    scr("s_og", [512, L], BF16)
    scr("s_xg", [D, L], BF16)
    scr("s_glu", [2816, L], BF16)
    scr("s_g", [4096, L], BF16)
    K.out["out_fm"] = nc.dram_tensor("out_fm", [D, L], F32, kind="ExternalOutput").ap()


def init_consts(K):
    nc = K.nc
    K.consts["ident_bf"] = K.sb("g_ident_bf", [128, 128], BF16, phase=False)
    K.consts["ident_f"] = K.sb("g_ident_f", [128, 128], F32, phase=False)
    K.consts["mask_geq"] = K.sb("g_mask_geq", [128, 128], F32, phase=False)
    K.consts["mask_leq"] = K.sb("g_mask_leq", [128, 128], F32, phase=False)
    K.consts["pos"] = K.sb("g_pos", [1, L], F32, phase=False)
    K.consts["ones_bf_col"] = K.sb("g_ones_bf_col", [128, 1], BF16, phase=False)
    K.consts["ones_f_row"] = K.sb("g_ones_f_row", [1, 128], F32, phase=False)
    K.consts["negones_f_row"] = K.sb("g_negones_f_row", [1, 128], F32, phase=False)
    K.consts["eights_f_row"] = K.sb("g_eights_f_row", [1, 128], F32, phase=False)
    K.consts["ones_bf_row"] = K.sb("g_ones_bf_row", [1, TT], BF16, phase=False)
    K.consts["r128_bf"] = K.sb("g_r128_bf", [128, 128], BF16, phase=False)
    K.consts["ones128_f"] = K.sb("g_ones128_f", [128, 128], F32, phase=False)
    K.load(K.consts["ident_bf"][:, :], K.inp["c_ident"], cast=True)
    K.load(K.consts["ident_f"][:, :], K.inp["c_ident"])
    K.load(K.consts["mask_geq"][:, :], K.inp["c_mask_geq"])
    K.load(K.consts["mask_leq"][:, :], K.inp["c_mask_leq"])
    K.load(K.consts["pos"][:, :], K.inp["c_pos"])
    K.load(K.consts["r128_bf"][:, :], K.inp["c_r128"], cast=True)
    cvals = [0.0, 1.0, -1.0, -0.5, 0.5, 1e-5, 1e-6, 0.125, -1e30]
    ctile = K.sb("g_constvals", [128, len(cvals)], F32, phase=False)
    for i, v in enumerate(cvals):
        K.dve.memset(ctile[:, i:i + 1], v)
        nc.const_aps.register(F32, v, ctile[:, i:i + 1])
    K.dve.memset(K.consts["ones128_f"][:, :], 1.0)
    K.dve.memset(K.consts["ones_bf_col"][:, :], 1.0)
    K.dve.memset(K.consts["ones_f_row"][:, :], 1.0)
    K.dve.memset(K.consts["negones_f_row"][:, :], -1.0)
    K.dve.memset(K.consts["eights_f_row"][:, :], 8.0)
    K.sem["dve"].inc(K.dve.memset(K.consts["ones_bf_row"][:, :], 1.0))



def emit_sandwich_ln(K, y_dram, x_old_dram, z_dram, xn_bf_dram, xn_f32_dram=None,
                     pass_b_only=False):
    ysb = K.sb("ln_ysb", [128, DC * TT], F32)
    xosb = K.sb("ln_xosb", [128, DC * TT], F32) if (x_old_dram is not None) else None
    zsb = K.sb("ln_zsb", [128, DC * TT], F32)
    zsb2 = K.sb("ln_zsb2", [128, DC * TT], F32)
    xnsb = K.sb("ln_xnsb", [128, DC * TT], BF16) if xn_bf_dram is not None else None
    xnfsb = K.sb("ln_xnfsb", [128, DC * TT], F32) if xn_f32_dram is not None else None
    bfsb = K.sb("ln_bfsb", [128, DC * TT], BF16)
    sqsb = K.sb("ln_sqsb", [128, DC * TT], BF16)
    rows = K.sb("ln_rows", [1, 6 * TT], F32)
    pp_stat = K.pp("ln_ps", [1, TT], n=4)
    pp_bc = K.pp("ln_pb", [128, TT], n=4)
    onesb = K.consts["ones_bf_col"]
    ones_f = K.consts["ones_f_row"]
    negones_f = K.consts["negones_f_row"]

    def ln_pass(src_sb, out_cb, t0):
        if DEV_STAGE < 2:
            return
        K.sem["dve"].wait(K.act)
        K.sem["dve"].inc(K.dve.tensor_copy(bfsb[:, :], src_sb[:, :]))
        K.sem["act"].inc(K.act.activation(sqsb[:, :], src_sb[:, :], AF.Square))
        if DEV_STAGE < 3:
            return
        K.sem["dve"].wait(K.pe)
        K.sem["act"].wait(K.pe)
        ps_sum = pp_stat.take()
        for c in range(DC):
            K.pe.matmul(ps_sum[:, :], onesb[:, :], bfsb[:, c * TT:(c + 1) * TT],
                        start=(c == 0), stop=(c == DC - 1))
        ps_sq = pp_stat.take()
        last = None
        for c in range(DC):
            last = K.pe.matmul(ps_sq[:, :], onesb[:, :], sqsb[:, c * TT:(c + 1) * TT],
                               start=(c == 0), stop=(c == DC - 1))
        K.sem["pe"].inc(last)
        if DEV_STAGE < 4:
            return
        K.sem["pe"].wait(K.dve)
        mean = rows[0:1, 0:TT]
        var = rows[0:1, TT:2 * TT]
        rstd = rows[0:1, 2 * TT:3 * TT]
        negmean = rows[0:1, 3 * TT:4 * TT]
        m2 = rows[0:1, 4 * TT:5 * TT]
        pp_stat.done(K.dve.tensor_scalar(mean, ps_sum[:, :], 1.0 / D, None, ALU.mult))
        inst = K.dve.tensor_scalar(var, ps_sq[:, :], 1.0 / D, None, ALU.mult)
        K.sem["dve"].inc(inst)
        pp_done_real(K, pp_stat, inst)
        K.sem["dve"].wait(K.dve)
        K.sem["dve"].inc(K.dve.tensor_mul(m2, mean, mean))
        K.sem["dve"].wait(K.dve)
        K.sem["dve"].inc(K.dve.tensor_sub(var, var, m2))
        K.sem["dve"].wait(K.dve)
        K.sem["dve"].inc(K.dve.tensor_scalar(negmean, mean, -1.0, None, ALU.mult))
        lnv = rows[0:1, 5 * TT:6 * TT]
        K.sem["dve"].wait(K.act)
        K.sem["act"].inc(K.act.activation(lnv, var, AF.Ln, bias=1e-5))
        K.sem["act"].wait(K.act)
        K.sem["act"].inc(K.act.activation(rstd, lnv, AF.Exp, scale=-0.5))
        if DEV_STAGE < 5:
            return
        K.sem["act"].wait(K.pe)
        negmB = pp_bc.take()
        K.pe.matmul(negmB[:, :], ones_f[:, :], negmean, start=True, stop=True)
        rstdB = pp_bc.take()
        K.sem["pe"].inc(K.pe.matmul(rstdB[:, :], ones_f[:, :], rstd, start=True, stop=True))
        K.sem["pe"].wait(K.dve)
        if DEV_STAGE < 6:
            pp_bc.done(K.dve.nop(), 2)
            return
        out_cb(negmB, rstdB)

    for t in range(NT if DEV_TILES is None else DEV_TILES):
        t0 = t * TT
        K.sem["dve"].wait(K.sp)
        K.sem["pe"].wait(K.sp)
        K.load(ysb.ap().rearrange("p (c t) -> p c t", c=DC), K.tile_ap(y_dram, t0, TT))
        if xosb is not None:
            K.load(xosb.ap().rearrange("p (c t) -> p c t", c=DC), K.tile_ap(x_old_dram, t0, TT))
        K.wait_loads(K.dve)
        K.wait_loads(K.act)
        K.wait_loads(K.pe)
        K.wait_stores(K.dve)
        K.sem["pe"].wait(K.dve)
        K.sem["pe"].wait(K.act)

        if not pass_b_only:
            def cb_a(negmB, rstdB):
                last = None
                for c in range(DC):
                    cs = slice(c * TT, (c + 1) * TT)
                    K.dve.tensor_add(zsb[:, cs], ysb[:, cs], negmB[:, :])
                    K.dve.tensor_mul(zsb[:, cs], zsb[:, cs], rstdB[:, :])
                    last = K.dve.tensor_add(zsb[:, cs], zsb[:, cs], xosb[:, cs])
                K.sem["dve"].inc(last)
                K.bump("dve", pp_bc.sem, 2)
                pp_bc.dones += 2
            ln_pass(ysb, cb_a, t0)
            if z_dram is not None and DEV_STAGE >= 7:
                K.sem["dve"].wait(K.sp)
                K.store(K.tile_ap(z_dram, t0, TT),
                        zsb.ap().rearrange("p (c t) -> p c t", c=DC), eng=K.sp)
            src_b = zsb
        else:
            src_b = ysb

        if xn_bf_dram is not None or xn_f32_dram is not None:
            def cb_b(negmB, rstdB):
                last = None
                for c in range(DC):
                    cs = slice(c * TT, (c + 1) * TT)
                    if xnfsb is not None:
                        K.dve.tensor_add(xnfsb[:, cs], src_b[:, cs], negmB[:, :])
                        last = K.dve.tensor_mul(xnfsb[:, cs], xnfsb[:, cs], rstdB[:, :])
                        if xnsb is not None:
                            last = K.dve.tensor_copy(xnsb[:, cs], xnfsb[:, cs])
                    else:
                        K.dve.tensor_add(zsb2[:, cs], src_b[:, cs], negmB[:, :])
                        last = K.dve.tensor_mul(xnsb[:, cs], zsb2[:, cs], rstdB[:, :])
                K.sem["dve"].inc(last)
                K.bump("dve", pp_bc.sem, 2)
                pp_bc.dones += 2
            ln_pass(src_b, cb_b, t0)
            K.sem["dve"].wait(K.pool)
            K.sem["dve"].wait(K.sp)
            if DEV_STAGE < 7:
                continue
            if xn_bf_dram is not None:
                K.store(K.tile_ap(xn_bf_dram, t0, TT),
                        xnsb.ap().rearrange("p (c t) -> p c t", c=DC), eng=K.sp)
            if xn_f32_dram is not None:
                K.store(K.tile_ap(xn_f32_dram, t0, TT),
                        xnfsb.ap().rearrange("p (c t) -> p c t", c=DC), eng=K.sp)




def emit_proj(K, name, w_dram, Kd, N, x_dram, out_dram=None, out_dtype=BF16,
              bias_row_dram=None, x_cast=False, evac=None, resid_dram=None,
              out_scale=None, x_loader=None, xsb=None, pp=None, store_cb=None):
    KC = Kd // 128
    NC = (N + 127) // 128
    wsb = K.sb(f"{name}_w", [128, KC * N], BF16)
    for kc in range(KC):
        K.load(wsb[:, kc * N:(kc + 1) * N], w_dram[kc * 128:(kc + 1) * 128, :], cast=True)
    brow = None
    if bias_row_dram is not None:
        brow = K.sb(f"{name}_bias", [1, N], BF16)
        K.load(brow[:, :], bias_row_dram, cast=True)
    if xsb is None:
        xsb = K.sb(f"{name}_x", [128, KC * TT], BF16)
    osb = K.sb(f"{name}_o", [128, NC * TT], out_dtype)
    rsb = K.sb(f"{name}_r", [128, NC * TT], F32) if resid_dram is not None else None
    own_pp = pp is None
    if own_pp:
        pp = K.pp(f"{name}_pp", [128, TT], n=4)
    onesr = K.consts["ones_bf_row"]
    for t in range(NT):
        t0 = t * TT
        if x_loader is not None:
            x_loader(t)
        else:
            K.sem["pe"].wait(K.sp)
            K.sem["pe"].wait(K.pool)
            K.wait_stores(K.sp)
            K.load(xsb.ap().rearrange("p (c t) -> p c t", c=KC),
                   K.tile_ap(x_dram, t0, TT, chunks=KC), cast=x_cast)
        if rsb is not None:
            K.sem["dve"].wait(K.sp)
            K.load(rsb.ap().rearrange("p (c t) -> p c t", c=NC),
                   K.tile_ap(resid_dram, t0, TT, chunks=NC))
        K.wait_loads(K.pe)
        for ncx in range(NC):
            nw = min(128, N - ncx * 128)
            ps = pp.take()
            first = True
            if brow is not None:
                K.pe.matmul(ps[0:nw, :], brow[0:1, ncx * 128:ncx * 128 + nw],
                            onesr[:, :], start=True, stop=False)
                first = False
            last = None
            for kc in range(KC):
                last = K.pe.matmul(ps[0:nw, :],
                                   wsb[:, kc * N + ncx * 128: kc * N + ncx * 128 + nw],
                                   xsb[:, kc * TT:(kc + 1) * TT],
                                   start=first, stop=(kc == KC - 1))
                first = False
            K.sem["pe"].inc(last)
            K.sem["pe"].wait(K.dve)
            cs = slice(ncx * TT, ncx * TT + TT)
            if evac is not None:
                evac(ncx, nw, t, ps, osb, pp)
            elif rsb is not None:
                inst = K.dve.tensor_add(osb[0:nw, cs], ps[0:nw, :], rsb[0:nw, cs])
                K.sem["dve"].inc(inst)
                pp_done_real(K, pp, inst)
            elif out_scale is not None:
                inst = K.dve.tensor_scalar(osb[0:nw, cs], ps[0:nw, :], out_scale,
                                           None, ALU.mult)
                K.sem["dve"].inc(inst)
                pp_done_real(K, pp, inst)
            else:
                inst = K.dve.tensor_copy(osb[0:nw, cs], ps[0:nw, :])
                K.sem["dve"].inc(inst)
                pp_done_real(K, pp, inst)
        if store_cb is not None:
            store_cb(t, osb)
        elif out_dram is not None:
            K.sem["dve"].wait(K.sp)
            K.sem["act"].wait(K.sp)
            K.store(K.tile_ap(out_dram, t0, TT, chunks=NC),
                    osb.ap().rearrange("p (c t) -> p c t", c=NC), eng=K.sp)



def emit_l0attn(K):
    old_es = K.phase_es
    beta_tm = K.sb("l0a_beta_tm", [128, 32 * 16], F32)
    fw_tm = K.sb("l0a_fw_tm", [128, 32 * 16], F32)
    sw_tm = K.sb("l0a_sw_tm", [128, 32 * 16], F32)
    invB = K.sb("l0a_invB", [128, 32], F32)
    btf1B = K.sb("l0a_btf1B", [128, 32], F32)
    w2B = K.sb("l0a_w2B", [128, 32], F32)
    btf2B = K.sb("l0a_btf2B", [128, 1], F32)
    wtf1 = K.sb("l0a_wtf1", [128, 32], BF16)
    K.maps_b = (beta_tm, fw_tm, sw_tm, invB, btf1B, w2B, btf2B)
    K.wtf1 = wtf1
    K.load(wtf1[0:64, :], K.inp["w_l0a_Wtf1"], cast=True)
    K.load(wtf1[64:128, :], K.inp["w_l0a_Wtf1"], cast=True)
    with ExitStack() as es_ab:
        K.phase_es = es_ab
        beta_fm = K.sb("l0a_beta", [16, L], F32)
        fd_fm = K.sb("l0a_fd", [16, L], F32)
        sd_fm = K.sb("l0a_sd", [16, L], F32)
        with ExitStack() as es_a1:
            K.phase_es = es_a1
            emit_l0attn_proj(K, beta_fm, fd_fm, sd_fm)
            K.phase_es = es_ab
        K.barrier()
        with ExitStack() as es_b1:
            K.phase_es = es_b1
            emit_l0attn_decay(K, beta_fm, fd_fm, sd_fm)
            K.phase_es = es_ab
        K.barrier()
    K.phase_es = old_es
    with ExitStack() as es_c:
        K.phase_es = es_c
        emit_l0attn_pairs(K)
        K.phase_es = old_es
    K.barrier()
    with ExitStack() as es_d:
        K.phase_es = es_d
        emit_proj(K, "l0a_wo", K.inp["w_l0a_Wo"], D, D, K.scr["s_o"],
                  out_dram=K.scr["s_a"], out_dtype=F32)
        K.phase_es = old_es


def emit_l0attn_proj(K, beta_fm, fd_fm, sd_fm):
    xnb = K.scr["s_xnb"]
    brow2 = K.sb("l0a_brow2", [1, 32], F32)
    bcol2 = K.sb("l0a_bcol2", [16, 2], F32)
    K.load(brow2[0:1, 0:16], K.inp["w_l0a_bfd"])
    K.load(brow2[0:1, 16:32], K.inp["w_l0a_bsd"])
    ppf = K.pp("l0a_ppA", [128, TT], n=4)
    K.wait_loads(K.pe)
    for i in range(2):
        ps = ppf.take()
        inst = K.pe.transpose(ps[0:16, 0:1], brow2[0:1, 16 * i:16 * i + 16],
                              K.consts["ones_f_row"][0:1, 0:1])
        K.sem["pe"].inc(inst)
        K.sem["pe"].wait(K.dve)
        inst = K.dve.tensor_copy(bcol2[:, i:i + 1], ps[0:16, 0:1])
        K.sem["dve"].inc(inst)
        pp_done_real(K, ppf, inst)
    wq = K.sb("l0a_wq", [128, DC * D], BF16)
    wk = K.sb("l0a_wk", [128, DC * D], BF16)
    wv = K.sb("l0a_wv", [128, DC * D], BF16)
    wmap = K.sb("l0a_wmap", [128, DC * 48], BF16)
    for kc in range(DC):
        K.load(wq[:, kc * D:(kc + 1) * D], K.inp["w_l0a_Wq"][kc * 128:(kc + 1) * 128, :], cast=True)
        K.load(wk[:, kc * D:(kc + 1) * D], K.inp["w_l0a_Wk"][kc * 128:(kc + 1) * 128, :], cast=True)
        K.load(wv[:, kc * D:(kc + 1) * D], K.inp["w_l0a_Wv"][kc * 128:(kc + 1) * 128, :], cast=True)
        K.load(wmap[:, kc * 48 + 0:kc * 48 + 16], K.inp["w_l0a_Wb"][kc * 128:(kc + 1) * 128, :], cast=True)
        K.load(wmap[:, kc * 48 + 16:kc * 48 + 32], K.inp["w_l0a_Wfd"][kc * 128:(kc + 1) * 128, :], cast=True)
        K.load(wmap[:, kc * 48 + 32:kc * 48 + 48], K.inp["w_l0a_Wsd"][kc * 128:(kc + 1) * 128, :], cast=True)
    xsb = K.sb("l0a_x", [128, DC * TT], BF16)
    osb = K.sb("l0a_osb", [128, 3 * DC * TT], BF16)
    for t in range(NT):
        t0 = t * TT
        K.sem["pe"].wait(K.sp)
        K.wait_stores(K.sp)
        K.load(xsb.ap().rearrange("p (c t) -> p c t", c=DC), K.tile_ap(xnb, t0, TT))
        K.wait_loads(K.pe)
        for pi, w in enumerate((wq, wk, wv)):
            for ncx in range(DC):
                ps = ppf.take()
                last = None
                for kc in range(DC):
                    last = K.pe.matmul(ps[:, :],
                                       w[:, kc * D + ncx * 128:kc * D + ncx * 128 + 128],
                                       xsb[:, kc * TT:(kc + 1) * TT],
                                       start=(kc == 0), stop=(kc == DC - 1))
                K.sem["pe"].inc(last)
                K.sem["pe"].wait(K.dve)
                inst = K.dve.tensor_copy(osb[:, (pi * DC + ncx) * TT:(pi * DC + ncx + 1) * TT],
                                         ps[:, :])
                K.sem["dve"].inc(inst)
                pp_done_real(K, ppf, inst)
        for mi, (mfm, bias) in enumerate(((beta_fm, None), (fd_fm, 0), (sd_fm, 1))):
            ps = ppf.take()
            last = None
            for kc in range(DC):
                last = K.pe.matmul(ps[0:16, :],
                                   wmap[:, kc * 48 + mi * 16:kc * 48 + mi * 16 + 16],
                                   xsb[:, kc * TT:(kc + 1) * TT],
                                   start=(kc == 0), stop=(kc == DC - 1))
            K.sem["pe"].inc(last)
            K.sem["pe"].wait(K.act)
            if bias is None:
                inst = K.act.activation(mfm[0:16, t0:t0 + TT], ps[0:16, :], AF.Sigmoid)
            else:
                inst = K.act.activation(mfm[0:16, t0:t0 + TT], ps[0:16, :], AF.Sigmoid,
                                        bias=bcol2[:, bias:bias + 1])
            K.sem["act"].inc(inst)
            pp_done_real(K, ppf, inst, eng="act")
        K.sem["dve"].wait(K.sp)
        for pi, dst in enumerate(("s_q", "s_k", "s_v")):
            K.store(K.tile_ap(K.scr[dst], t0, TT),
                    osb[:, pi * DC * TT:(pi + 1) * DC * TT].rearrange("p (c t) -> p c t", c=DC),
                    eng=K.sp)


def emit_l0attn_decay(K, beta_fm, fd_fm, sd_fm):
    beta_tm, fw_tm, sw_tm, invB, btf1B, w2B, btf2B = K.maps_b
    logf = K.sb("l0b_logf", [16, L], F32)
    cs_f = K.sb("l0b_cs", [16, L], F32)
    fw_fm = K.sb("l0b_fwfm", [16, L], F32)
    sw_fm = K.sb("l0b_swfm", [16, L], F32)
    zeros16 = K.sb("l0b_z16", [16, L], F32)
    sums = K.sb("l0b_sums", [16, 4], F32)
    rowtmp = K.sb("l0b_rowtmp", [1, 64], F32)
    brow3 = K.sb("l0b_brow3", [1, 33], F32)
    ppf2 = K.pp("l0b_pp", [128, TT], n=4)
    K.sem["dve"].inc(K.dve.memset(zeros16[:, :], 0.0))
    for si, (src_map, dst_fm, sumcol) in enumerate(((fd_fm, fw_fm, 0), (sd_fm, sw_fm, 2))):
        K.sem["act"].inc(K.act.activation(logf[:, :], src_map[0:16, :], AF.Ln, bias=1e-6))
        K.sem["act"].wait(K.dve)
        K.sem["dve"].inc(K.dve.tensor_tensor_scan(cs_f[:, :], logf[:, :], zeros16[:, :],
                                                  0.0, ALU.add, ALU.add))
        K.sem["dve"].wait(K.act)
        K.sem["act"].inc(K.act.activation(dst_fm[:, :], cs_f[:, :], AF.Exp, scale=-1.0,
                                          bias=cs_f[0:16, L - 1:L],
                                          accum_out=sums[:, sumcol:sumcol + 1]))
        K.sem["act"].wait(K.dve)
        K.sem["dve"].inc(K.dve.tensor_scalar(sums[:, sumcol + 1:sumcol + 2],
                                             sums[:, sumcol:sumcol + 1], 1e-6, None, ALU.add))
        K.sem["dve"].wait(K.dve)
        K.sem["dve"].inc(K.dve.reciprocal(sums[:, sumcol + 1:sumcol + 2],
                                          sums[:, sumcol + 1:sumcol + 2]))
    K.sem["dve"].wait(K.pe)
    K.sem["act"].wait(K.pe)
    for mfm, mtm in ((beta_fm, beta_tm), (fw_fm, fw_tm), (sw_fm, sw_tm)):
        for c in range(32):
            ps = ppf2.take()
            inst = K.pe.transpose(ps[:, 0:16], mfm[0:16, c * 128:(c + 1) * 128],
                                  K.consts["ident_f"][0:16, 0:16])
            K.sem["pe"].inc(inst)
            K.sem["pe"].wait(K.dve)
            inst = K.dve.tensor_copy(mtm[:, c * 16:(c + 1) * 16], ps[:, 0:16])
            K.sem["dve"].inc(inst)
            pp_done_real(K, ppf2, inst)
    for si in range(2):
        ps = ppf2.take()
        inst = K.pe.transpose(ps[0:1, 0:16], sums[0:16, 2 * si + 1:2 * si + 2],
                              K.consts["ident_f"][0:16, 0:16])
        K.sem["pe"].inc(inst)
        K.sem["pe"].wait(K.dve)
        inst = K.dve.tensor_copy(rowtmp[0:1, 16 * si:16 * si + 16], ps[0:1, 0:16])
        K.sem["dve"].inc(inst)
        pp_done_real(K, ppf2, inst)
        K.sem["dve"].wait(K.pe)
        ps = ppf2.take()
        inst = K.pe.matmul(ps[:, 0:16], K.consts["ones_f_row"][:, :],
                           rowtmp[0:1, 16 * si:16 * si + 16], start=True, stop=True)
        K.sem["pe"].inc(inst)
        K.sem["pe"].wait(K.dve)
        inst = K.dve.tensor_copy(invB[:, 16 * si:16 * si + 16], ps[:, 0:16])
        K.sem["dve"].inc(inst)
        pp_done_real(K, ppf2, inst)
    K.load(rowtmp[0:1, 32:64], K.inp["w_l0a_btf1"])
    K.load(brow3[0:1, 0:32], K.inp["w_l0a_wtf2"])
    K.load(brow3[0:1, 32:33], K.inp["w_l0a_btf2"])
    K.wait_loads(K.pe)
    for dst, rowap, w in ((btf1B, rowtmp[0:1, 32:64], 32), (w2B, brow3[0:1, 0:32], 32),
                          (btf2B, brow3[0:1, 32:33], 1)):
        ps = ppf2.take()
        inst = K.pe.matmul(ps[:, 0:w], K.consts["ones_f_row"][:, :], rowap,
                           start=True, stop=True)
        K.sem["pe"].inc(inst)
        K.sem["pe"].wait(K.dve)
        inst = K.dve.tensor_copy(dst[:, 0:w], ps[:, 0:w])
        K.sem["dve"].inc(inst)
        pp_done_real(K, ppf2, inst)


def pp_done_real(K, pp, inst, eng="dve", k=1):
    K.bump(eng, pp.sem, k)
    pp.dones += k


def emit_l0attn_pairs(K):
    beta_tm, fw_tm, sw_tm, invB, btf1B, w2B, btf2B = K.maps_b
    q_p = K.sb("l0c_q", [128, L], BF16)
    k_p = K.sb("l0c_k", [128, L], BF16)
    v_p = K.sb("l0c_v", [128, L], BF16)
    k_tm = K.sb("l0c_ktm", [128, L], BF16)
    v_tm = K.sb("l0c_vtm", [128, L], BF16)
    sqq = K.sb("l0c_sqq", [128, L], BF16)
    sqk = K.sb("l0c_sqk", [128, L], BF16)
    rowsb = K.sb("l0c_row", [1, TT], F32)
    rq_raw = K.sb("l0c_rq", [128, 64], F32)
    rk_raw = K.sb("l0c_rk", [128, 64], F32)
    ecol = K.sb("l0c_ecol", [128, 64 * 3], F32)
    kf = K.sb("l0c_kf", [128, L], BF16)
    ks2 = K.sb("l0c_ks2", [128, L], BF16)
    vs = K.sb("l0c_vs", [128, L], BF16)
    pre_fm = K.sb("l0c_prefm", [32, 2 * L], F32)
    pre_tm = K.sb("l0c_pretm", [128, 2048], F32)
    silu_sb = K.sb("l0c_silu", [128, 2048], F32)
    tfc = K.sb("l0c_tfc", [128, 64], F32)
    c1t = K.sb("l0c_c1", [128, 64], F32)
    c2t = K.sb("l0c_c2", [128, 64], F32)
    msb = K.sb("l0c_m", [128, 128], BF16)
    o_tm = K.sb("l0c_otm", [128, L], BF16)
    o_tm2 = K.sb("l0c_otm2", [128, L], BF16)
    o_fm = K.sb("l0c_ofm", [128, L], BF16)
    ppf = K.pp("l0c_ppf", [128, TT], n=4)
    ppb = K.pp("l0c_ppb", [128, 128], n=2, dtype=BF16)
    ppm = K.pp("l0c_ppm", [128, 64], n=2)
    onesb = K.consts["ones_bf_col"]
    identb = K.consts["ident_bf"]
    for p in range(8):
        for eng in (K.sp,):
            K.sem["pe"].wait(eng)
            K.sem["dve"].wait(eng)
            K.sem["act"].wait(eng)
            K.wait_stores(eng)
        K.load(q_p[:, :], K.scr["s_q"][p * 128:(p + 1) * 128, :])
        K.load(k_p[:, :], K.scr["s_k"][p * 128:(p + 1) * 128, :])
        K.load(v_p[:, :], K.scr["s_v"][p * 128:(p + 1) * 128, :])
        K.wait_loads(K.pe)
        K.wait_loads(K.act)
        for src, dst in ((k_p, k_tm), (v_p, v_tm)):
            for c in range(32):
                ps = ppb.take()
                inst = K.pe.transpose(ps[:, :], src[:, c * 128:(c + 1) * 128], identb[:, :])
                K.sem["pe"].inc(inst)
                K.sem["pe"].wait(K.dve)
                inst = K.dve.tensor_copy(dst[:, c * 128:(c + 1) * 128], ps[:, :])
                K.sem["dve"].inc(inst)
                pp_done_real(K, ppb, inst)
        K.sem["act"].inc(K.act.activation(sqq[:, :], q_p[:, :], AF.Square))
        K.sem["act"].inc(K.act.activation(sqk[:, :], k_p[:, :], AF.Square))
        K.sem["act"].wait(K.pe)
        for sq, raw in ((sqq, rq_raw), (sqk, rk_raw)):
            for hh in range(2):
                for t in range(NT):
                    ps = ppf.take()
                    inst = K.pe.matmul(ps[0:1, :], onesb[hh * 64:(hh + 1) * 64, 0:1],
                                       sq[hh * 64:(hh + 1) * 64, t * TT:(t + 1) * TT],
                                       start=True, stop=True)
                    K.sem["pe"].inc(inst)
                    K.sem["pe"].wait(K.dve)
                    inst = K.dve.tensor_copy(rowsb[0:1, :], ps[0:1, :])
                    K.sem["dve"].inc(inst)
                    pp_done_real(K, ppf, inst)
                    K.sem["dve"].wait(K.pe)
                    for cc in range(4):
                        c = t * 4 + cc
                        ps2 = ppf.take()
                        inst = K.pe.transpose(ps2[:, 0:1], rowsb[0:1, cc * 128:(cc + 1) * 128],
                                              K.consts["ones_f_row"][0:1, 0:1])
                        K.sem["pe"].inc(inst)
                        K.sem["pe"].wait(K.dve)
                        inst = K.dve.tensor_copy(raw[:, c * 2 + hh:c * 2 + hh + 1], ps2[:, 0:1])
                        K.sem["dve"].inc(inst)
                        pp_done_real(K, ppf, inst)
        K.sem["dve"].wait(K.dve)
        for raw in (rq_raw, rk_raw):
            K.sem["dve"].inc(K.dve.tensor_scalar(raw[:, :], raw[:, :], 1e-24, None, ALU.max))
            K.sem["dve"].wait(K.act)
            K.sem["act"].inc(K.act.activation(raw[:, :], raw[:, :], AF.Ln))
            K.sem["act"].wait(K.act)
            K.sem["act"].inc(K.act.activation(raw[:, :], raw[:, :], AF.Exp, scale=-0.5))
        K.sem["act"].wait(K.dve)
        K.sem["act"].wait(K.pe)
        for hh in range(2):
            for t in range(NT):
                ps = ppf.take()
                inst = K.pe.matmul(ps[0:32, :], K.wtf1[hh * 64:(hh + 1) * 64, :],
                                   k_p[hh * 64:(hh + 1) * 64, t * TT:(t + 1) * TT],
                                   start=True, stop=True)
                K.sem["pe"].inc(inst)
                K.sem["pe"].wait(K.dve)
                inst = K.dve.tensor_copy(pre_fm[0:32, hh * L + t * TT:hh * L + (t + 1) * TT],
                                         ps[0:32, :])
                K.sem["dve"].inc(inst)
                pp_done_real(K, ppf, inst)
        K.sem["dve"].wait(K.pe)
        for c in range(32):
            for hh in range(2):
                ps = ppf.take()
                inst = K.pe.transpose(ps[:, 0:32],
                                      pre_fm[0:32, hh * L + c * 128:hh * L + (c + 1) * 128],
                                      K.consts["ident_f"][0:32, 0:32])
                K.sem["pe"].inc(inst)
                K.sem["pe"].wait(K.dve)
                inst = K.dve.tensor_copy(pre_tm[:, (c * 2 + hh) * 32:(c * 2 + hh + 1) * 32],
                                         ps[:, 0:32])
                K.sem["dve"].inc(inst)
                pp_done_real(K, ppf, inst)
        last = None
        for c in range(32):
            for hh in range(2):
                h = 2 * p + hh
                e = (c * 2 + hh) * 3
                last = K.dve.tensor_mul(ecol[:, e:e + 1], rk_raw[:, c * 2 + hh:c * 2 + hh + 1],
                                        beta_tm[:, c * 16 + h:c * 16 + h + 1])
        K.sem["dve"].inc(last)
        K.sem["dve"].wait(K.dve)
        last = None
        for c in range(32):
            for hh in range(2):
                h = 2 * p + hh
                e = (c * 2 + hh) * 3
                K.dve.tensor_mul(ecol[:, e + 1:e + 2], ecol[:, e:e + 1],
                                 fw_tm[:, c * 16 + h:c * 16 + h + 1])
                last = K.dve.tensor_mul(ecol[:, e + 2:e + 3], ecol[:, e:e + 1],
                                        sw_tm[:, c * 16 + h:c * 16 + h + 1])
        K.sem["dve"].inc(last)
        K.sem["dve"].wait(K.dve)
        last = None
        for c in range(32):
            for hh in range(2):
                h = 2 * p + hh
                e = (c * 2 + hh) * 3
                sl = slice(c * 128 + hh * 64, c * 128 + hh * 64 + 64)
                K.dve.tensor_scalar(kf[:, sl], k_tm[:, sl], ecol[:, e + 1:e + 2], None, ALU.mult)
                K.dve.tensor_scalar(ks2[:, sl], k_tm[:, sl], ecol[:, e + 2:e + 3], None, ALU.mult)
                K.dve.tensor_scalar(vs[:, sl], v_tm[:, sl],
                                    beta_tm[:, c * 16 + h:c * 16 + h + 1], None, ALU.mult)
                ts_ = slice((c * 2 + hh) * 32, (c * 2 + hh + 1) * 32)
                last = K.dve.tensor_scalar(pre_tm[:, ts_], pre_tm[:, ts_], ecol[:, e:e + 1],
                                           None, ALU.mult)
        K.sem["dve"].inc(last)
        K.sem["dve"].wait(K.dve)
        last = None
        for i in range(64):
            last = K.dve.tensor_add(pre_tm[:, i * 32:(i + 1) * 32],
                                    pre_tm[:, i * 32:(i + 1) * 32], btf1B[:, 0:32])
        K.sem["dve"].inc(last)
        K.sem["dve"].wait(K.act)
        K.sem["act"].inc(K.act.activation(silu_sb[:, :], pre_tm[:, :], AF.Silu))
        K.sem["act"].wait(K.dve)
        last = None
        for i in range(64):
            last = K.dve.tensor_mul(silu_sb[:, i * 32:(i + 1) * 32],
                                    silu_sb[:, i * 32:(i + 1) * 32], w2B[:, 0:32])
        K.sem["dve"].inc(last)
        K.sem["dve"].wait(K.dve)
        last = None
        for i in range(64):
            last = K.dve.reduce_sum(tfc[:, i:i + 1], silu_sb[:, i * 32:(i + 1) * 32], axis=AX.X)
        K.sem["dve"].inc(last)
        K.sem["dve"].wait(K.act)
        K.sem["act"].inc(K.act.activation(tfc[:, :], tfc[:, :], AF.Sigmoid, bias=btf2B[:, 0:1]))
        K.sem["act"].wait(K.dve)
        K.sem["dve"].inc(K.dve.tensor_scalar(tfc[:, :], tfc[:, :], 0.99, None, ALU.min))
        K.sem["dve"].wait(K.dve)
        K.sem["dve"].inc(K.dve.tensor_scalar(tfc[:, :], tfc[:, :], 0.01, None, ALU.max))
        K.sem["dve"].wait(K.dve)
        K.sem["dve"].inc(K.dve.tensor_scalar(c1t[:, :], tfc[:, :], 0.3, 0.5, ALU.mult, ALU.add))
        K.sem["dve"].wait(K.dve)
        K.sem["dve"].inc(K.dve.tensor_scalar(c2t[:, :], c1t[:, :], -1.0, 1.0, ALU.mult, ALU.add))
        K.sem["dve"].wait(K.dve)
        K.dve.tensor_mul(c1t[:, :], c1t[:, :], rq_raw[:, :])
        K.sem["dve"].inc(K.dve.tensor_mul(c2t[:, :], c2t[:, :], rq_raw[:, :]))
        K.sem["dve"].wait(K.dve)
        last = None
        for hh in range(2):
            h = 2 * p + hh
            K.dve.tensor_scalar(c1t[:, hh::2], c1t[:, hh::2], invB[:, h:h + 1], None, ALU.mult)
            last = K.dve.tensor_scalar(c2t[:, hh::2], c2t[:, hh::2],
                                       invB[:, 16 + h:16 + h + 1], None, ALU.mult)
        K.sem["dve"].inc(last)
        K.sem["dve"].wait(K.pe)
        K.sem["dve"].wait(K.dve)
        for hh in range(2):
            for mi, src in enumerate((kf, ks2)):
                ps = ppm.take()
                last = None
                for c in range(32):
                    sl = slice(c * 128 + hh * 64, c * 128 + hh * 64 + 64)
                    last = K.pe.matmul(ps[0:64, :], src[:, sl], vs[:, sl],
                                       start=(c == 0), stop=(c == 31))
                K.sem["pe"].inc(last)
                K.sem["pe"].wait(K.dve)
                inst = K.dve.tensor_copy(msb[hh * 64:(hh + 1) * 64, mi * 64:(mi + 1) * 64],
                                         ps[0:64, :])
                K.sem["dve"].inc(inst)
                pp_done_real(K, ppm, inst)
        if DEV_DBGPAIR:
            K.sem["dve"].wait(K.pool)
            K.store(K.scr["s_y"][0:128, p * 128:(p + 1) * 128], msb[:, :])
            K.store(K.scr["s_y"][128:256, p * 64:(p + 1) * 64], c1t[:, :])
            K.store(K.scr["s_y"][256:384, p * 64:(p + 1) * 64], c2t[:, :])
            K.store(K.scr["s_y"][384:512, p * 192:(p + 1) * 192], ecol[:, :])
            K.store(K.scr["s_y"][512:640, p * 64:(p + 1) * 64], rq_raw[:, :])
        K.sem["dve"].wait(K.pe)
        for c in range(32):
            for hh in range(2):
                psf = ppm.take()
                K.pe.matmul(psf[:, :], q_p[hh * 64:(hh + 1) * 64, c * 128:(c + 1) * 128],
                            msb[hh * 64:(hh + 1) * 64, 0:64], start=True, stop=True)
                pso = ppm.take()
                inst = K.pe.matmul(pso[:, :], q_p[hh * 64:(hh + 1) * 64, c * 128:(c + 1) * 128],
                                   msb[hh * 64:(hh + 1) * 64, 64:128],
                                   start=True, stop=True)
                K.sem["pe"].inc(inst)
                K.sem["pe"].wait(K.dve)
                sl = slice(c * 128 + hh * 64, c * 128 + hh * 64 + 64)
                K.dve.tensor_scalar(o_tm[:, sl], psf[:, :], c1t[:, c * 2 + hh:c * 2 + hh + 1],
                                    None, ALU.mult)
                inst = K.dve.tensor_scalar(o_tm2[:, sl], pso[:, :],
                                           c2t[:, c * 2 + hh:c * 2 + hh + 1], None, ALU.mult)
                K.sem["dve"].inc(inst)
                pp_done_real(K, ppm, inst, k=2)
        K.sem["dve"].wait(K.dve)
        K.sem["dve"].inc(K.dve.tensor_add(o_tm[:, :], o_tm[:, :], o_tm2[:, :]))
        K.sem["dve"].wait(K.pe)
        for c in range(32):
            ps = ppb.take()
            inst = K.pe.transpose(ps[:, :], o_tm[:, c * 128:(c + 1) * 128], identb[:, :])
            K.sem["pe"].inc(inst)
            K.sem["pe"].wait(K.dve)
            inst = K.dve.tensor_copy(o_fm[:, c * 128:(c + 1) * 128], ps[:, :])
            K.sem["dve"].inc(inst)
            pp_done_real(K, ppb, inst)
        K.sem["dve"].wait(K.pool)
        K.store(K.scr["s_o"][p * 128:(p + 1) * 128, :], o_fm[:, :])




def _shift_loader(K, src_dram, xsb, KC):
    half = KC // 2
    def load(t):
        t0 = t * TT
        K.sem["pe"].wait(K.sp)
        K.sem["pe"].wait(K.pool)
        K.wait_stores(K.sp)
        K.load(xsb.ap().rearrange("p (c t) -> p c t", c=KC)[:, 0:half, :],
               K.tile_ap(src_dram, t0, TT, chunks=KC)[:, 0:half, :], cast=True)
        if t == 0:
            K.sem["pe"].wait(K.dve)
            for c in range(half, KC):
                K.sem["dve"].inc(K.dve.memset(xsb[:, c * TT:c * TT + 1], 0.0))
                K.load(xsb[:, c * TT + 1:(c + 1) * TT],
                       src_dram[c * 128:(c + 1) * 128, 0:TT - 1], cast=True)
            K.sem["dve"].wait(K.pe)
        else:
            K.load(xsb.ap().rearrange("p (c t) -> p c t", c=KC)[:, half:KC, :],
                   K.tile_ap(src_dram, t0 - 1, TT, chunks=KC)[:, half:KC, :], cast=True)
    return load


def emit_l0ffn_layer(K, l):
    xin = K.scr["s_xn"] if l == 0 else K.scr["s_y2"]
    yout2 = K.scr["s_y2"] if l == 0 else K.scr["s_a"]
    old_es = K.phase_es
    with ExitStack() as es:
        K.phase_es = es
        xsb = K.sb("ffA_x", [128, DC * TT], BF16)
        emit_proj(K, "ffA", K.inp[f"w_ff{l}_wqkv"], D, 1536, None,
                  out_dram=K.scr["s_qkv"], out_dtype=BF16,
                  x_loader=_shift_loader(K, xin, xsb, DC), xsb=xsb)
        K.phase_es = old_es
    K.barrier()
    with ExitStack() as es:
        K.phase_es = es
        emit_l0ffn_rotary(K, l)
        K.phase_es = old_es
    K.barrier()
    with ExitStack() as es:
        K.phase_es = es
        emit_l0ffn_attn(K, l)
        K.phase_es = old_es
    K.barrier()
    with ExitStack() as es:
        K.phase_es = es
        emit_proj(K, "ffD", K.inp[f"w_ff{l}_wo"], 512, D, K.scr["s_og"],
                  out_dram=K.scr["s_y"], out_dtype=F32, resid_dram=xin)
        K.phase_es = old_es
    K.barrier()
    with ExitStack() as es:
        K.phase_es = es
        emit_l0ffn_ln(K, l)
        K.phase_es = old_es
    K.barrier()
    with ExitStack() as es:
        K.phase_es = es
        emit_l0ffn_glu(K, l)
        K.phase_es = old_es
    K.barrier()
    with ExitStack() as es:
        K.phase_es = es
        emit_proj(K, "ffG", K.inp[f"w_ff{l}_w2"], 2816, D, K.scr["s_glu"],
                  out_dram=yout2, out_dtype=F32, resid_dram=K.scr["s_y"],
                  bias_row_dram=K.inp[f"w_ff{l}_b2"])
        K.phase_es = old_es


def emit_l0ffn_rotary(K, l):
    maps = {}
    cols = K.sb("ffB_cols", [128, 4], F32)
    rowt = K.sb("ffB_rowt", [1, 64], F32)
    ppf = K.pp("ffB_pp", [128, TT], n=4)
    for i, nm in enumerate(("qs", "qsp", "ks", "ksp")):
        K.sem["pe"].wait(K.sp)
        K.load(rowt[0:1, :], K.inp[f"w_ff{l}_{nm}"])
        K.wait_loads(K.pe)
        K.sem["dve"].wait(K.pe)
        ps = ppf.take()
        inst = K.pe.transpose(ps[0:64, 0:1], rowt[0:1, :], K.consts["ones_f_row"][0:1, 0:1])
        K.sem["pe"].inc(inst)
        K.sem["pe"].wait(K.dve)
        inst = K.dve.tensor_copy(cols[0:64, i:i + 1], ps[0:64, 0:1])
        K.sem["dve"].inc(inst)
        pp_done_real(K, ppf, inst)
        K.sem["dve"].wait(K.dve)
        K.sem["dve"].inc(K.dve.tensor_copy(cols[64:128, i:i + 1], cols[0:64, i:i + 1]))
    for nm, cst, sc in (("cq", "c_cq", 0), ("sq", "c_sq", 1), ("ck", "c_ck", 2), ("sk", "c_sk", 3)):
        m = K.sb(f"ffB_{nm}", [128, L], F32)
        maps[nm] = m
        K.load(m[0:64, :], K.inp[cst])
        K.load(m[64:128, :], K.inp[cst])
    K.wait_loads(K.dve)
    K.sem["dve"].wait(K.dve)
    last = None
    for nm, ci in (("cq", 0), ("sq", 1), ("ck", 2), ("sk", 3)):
        last = K.dve.tensor_scalar(maps[nm][:, :], maps[nm][:, :], cols[:, ci:ci + 1],
                                   None, ALU.mult)
    K.sem["dve"].inc(last)
    K.sem["dve"].wait(K.dve)
    qp = K.sb("ffB_qp", [128, L], BF16)
    sq_sb = K.sb("ffB_sq", [128, L], BF16)
    ssrow = K.sb("ffB_ssrow", [1, 2 * L], F32)
    t1 = K.sb("ffB_t1", [128, TT], F32)
    t2 = K.sb("ffB_t2", [128, TT], F32)
    out = K.sb("ffB_out", [128, L], BF16)
    for pp in range(8):
        is_q = pp < 4
        cmap = maps["cq"] if is_q else maps["ck"]
        smap = maps["sq"] if is_q else maps["sk"]
        dst = K.scr["s_qr"] if is_q else K.scr["s_kr"]
        drow = (pp % 4) * 128
        K.sem["pe"].wait(K.sp)
        K.sem["act"].wait(K.sp)
        K.sem["dve"].wait(K.sp)
        K.wait_stores(K.sp)
        K.load(qp[:, :], K.scr["s_qkv"][pp * 128:(pp + 1) * 128, :])
        K.wait_loads(K.act)
        K.sem["act"].inc(K.act.activation(sq_sb[:, :], qp[:, :], AF.Square))
        K.sem["act"].wait(K.pe)
        for hh in range(2):
            for t in range(NT):
                ps = ppf.take()
                inst = K.pe.matmul(ps[0:1, :], K.consts["ones_bf_col"][hh * 64:(hh + 1) * 64, 0:1],
                                   sq_sb[hh * 64:(hh + 1) * 64, t * TT:(t + 1) * TT],
                                   start=True, stop=True)
                K.sem["pe"].inc(inst)
                K.sem["pe"].wait(K.dve)
                inst = K.dve.tensor_copy(ssrow[0:1, hh * L + t * TT:hh * L + (t + 1) * TT],
                                         ps[0:1, :])
                K.sem["dve"].inc(inst)
                pp_done_real(K, ppf, inst)
        K.sem["dve"].wait(K.dve)
        K.sem["dve"].inc(K.dve.tensor_scalar(ssrow[:, :], ssrow[:, :], 1e-24, None, ALU.max))
        K.sem["dve"].wait(K.act)
        K.sem["act"].inc(K.act.activation(ssrow[:, :], ssrow[:, :], AF.Ln))
        K.sem["act"].wait(K.act)
        K.sem["act"].inc(K.act.activation(ssrow[:, :], ssrow[:, :], AF.Exp, scale=-0.5))
        K.sem["act"].wait(K.pe)
        K.sem["act"].wait(K.dve)
        for t in range(NT):
            tsl = slice(t * TT, (t + 1) * TT)
            psA = ppf.take()
            inst = K.pe.matmul(psA[:, :], K.consts["r128_bf"][:, :], qp[:, tsl],
                               start=True, stop=True)
            K.sem["pe"].inc(inst)
            psB = ppf.take()
            for hh in range(2):
                row = ssrow[0:1, hh * L + t * TT:hh * L + (t + 1) * TT]
                lhs = (K.consts["eights_f_row"] if is_q else K.consts["ones_f_row"])
                inst = K.pe.matmul(psB[hh * 64:(hh + 1) * 64, :], lhs[0:1, 0:64], row,
                                   start=True, stop=True)
            K.sem["pe"].inc(inst)
            K.sem["pe"].wait(K.dve)
            K.dve.tensor_mul(t1[:, :], cmap[:, tsl], qp[:, tsl])
            K.sem["dve"].inc(K.dve.tensor_mul(t2[:, :], smap[:, tsl], psA[:, :]))
            K.sem["dve"].wait(K.dve)
            K.sem["dve"].inc(K.dve.tensor_add(t1[:, :], t1[:, :], t2[:, :]))
            K.sem["dve"].wait(K.dve)
            inst = K.dve.tensor_mul(out[:, tsl], t1[:, :], psB[:, :])
            K.sem["dve"].inc(inst)
            pp_done_real(K, ppf, inst, k=2)
        K.sem["dve"].wait(K.sp)
        K.store(dst[drow:drow + 128, :], out[:, :], eng=K.sp)


def emit_l0ffn_attn(K, l):
    W = 512
    qp = K.sb("ffC_q", [128, L], BF16)
    kp = K.sb("ffC_k", [128, L], BF16)
    vp = K.sb("ffC_v", [128, L], BF16)
    v_tm = K.sb("ffC_vtm", [128, L], BF16)
    stash = K.sb("ffC_stash", [128, 640], F32)
    P_sb = K.sb("ffC_P", [128, 640], BF16)
    Dt = K.sb("ffC_D", [128, 128], BF16)
    PT = K.sb("ffC_PT", [128, 128], BF16)
    negm = K.sb("ffC_negm", [128, 1], F32)
    rs = K.sb("ffC_rs", [128, 2], F32)
    og = K.sb("ffC_og", [128, L], BF16)
    ppS = K.pp("ffC_ppS", [128, TT], n=2)
    ppT = K.pp("ffC_ppT", [128, 128], n=2, dtype=BF16)
    ppD = K.pp("ffC_ppD", [128, 128], n=2)
    ppO = K.pp("ffC_ppO", [64, 128], n=2)
    identb = K.consts["ident_bf"]
    for pr in range(4):
        K.sem["pe"].wait(K.sp)
        K.sem["act"].wait(K.sp)
        K.sem["dve"].wait(K.sp)
        K.wait_stores(K.sp)
        K.load(qp[:, :], K.scr["s_qr"][pr * 128:(pr + 1) * 128, :])
        K.load(kp[:, :], K.scr["s_kr"][pr * 128:(pr + 1) * 128, :])
        K.load(vp[:, :], K.scr["s_qkv"][1024 + pr * 128:1024 + (pr + 1) * 128, :])
        K.wait_loads(K.pe)
        for c in range(32):
            ps = ppT.take()
            inst = K.pe.transpose(ps[:, :], vp[:, c * 128:(c + 1) * 128], identb[:, :])
            K.sem["pe"].inc(inst)
            K.sem["pe"].wait(K.dve)
            inst = K.dve.tensor_copy(v_tm[:, c * 128:(c + 1) * 128], ps[:, :])
            K.sem["dve"].inc(inst)
            pp_done_real(K, ppT, inst)
        K.sem["dve"].wait(K.pe)
        for hh in range(2):
            hsl = slice(hh * 64, (hh + 1) * 64)
            for w in range(8):
                for r in range(4):
                    q0 = w * W + r * 128
                    wL = W - r * 128 if w > 0 else 0
                    wR = r * 128 + 128
                    Wt = wL + wR
                    kcn = Wt // 128
                    psL = None
                    if w > 0:
                        psL = ppS.take()
                        inst = K.pe.matmul(psL[:, 0:wL], qp[hsl, q0:q0 + 128],
                                           kp[hsl, (w - 1) * W + r * 128:w * W],
                                           start=True, stop=True)
                        K.sem["pe"].inc(inst)
                    psR = ppS.take()
                    inst = K.pe.matmul(psR[:, 0:wR], qp[hsl, q0:q0 + 128],
                                       kp[hsl, w * W:w * W + wR], start=True, stop=True)
                    K.sem["pe"].inc(inst)
                    K.sem["pe"].wait(K.dve)
                    if w > 0:
                        K.dve.tensor_add(stash[:, 0:128], psL[:, 0:128], K.consts["mask_geq"][:, :])
                        if wL > 128:
                            K.dve.tensor_copy(stash[:, 128:wL], psL[:, 128:wL])
                    if wR > 128:
                        K.dve.tensor_copy(stash[:, wL:wL + wR - 128], psR[:, 0:wR - 128])
                    inst = K.dve.tensor_add(stash[:, Wt - 128:Wt], psR[:, wR - 128:wR],
                                            K.consts["mask_leq"][:, :])
                    K.sem["dve"].inc(inst)
                    if w > 0:
                        pp_done_real(K, ppS, inst, k=2)
                    else:
                        pp_done_real(K, ppS, inst, k=1)
                    K.sem["dve"].wait(K.dve)
                    K.sem["dve"].inc(K.dve.reduce_max(negm[:, :], stash[:, 0:Wt], axis=AX.X,
                                                      negate=True))
                    K.sem["dve"].wait(K.act)
                    K.sem["act"].inc(K.act.activation(P_sb[:, 0:Wt], stash[:, 0:Wt], AF.Exp,
                                                      bias=negm[:, :], accum_out=rs[:, 0:1]))
                    K.sem["act"].wait(K.dve)
                    K.sem["dve"].inc(K.dve.reciprocal(rs[:, 1:2], rs[:, 0:1]))
                    K.sem["dve"].wait(K.dve)
                    K.sem["dve"].inc(K.dve.tensor_scalar(Dt[:, :], identb[:, :], rs[:, 1:2],
                                                         None, ALU.mult))
                    K.sem["dve"].wait(K.pe)
                    psO = ppO.take()
                    lastAV = None
                    for kc in range(kcn):
                        gc = (w - 1) * 4 + r + kc if (w > 0 and kc < wL // 128) else \
                             w * 4 + (kc - (wL // 128))
                        psT = ppD.take()
                        inst = K.pe.matmul(psT[:, :], P_sb[:, kc * 128:(kc + 1) * 128], Dt[:, :],
                                           start=True, stop=True)
                        K.sem["pe"].inc(inst)
                        K.sem["pe"].wait(K.dve)
                        inst = K.dve.tensor_copy(PT[:, :], psT[:, :])
                        K.sem["dve"].inc(inst)
                        pp_done_real(K, ppD, inst)
                        K.sem["dve"].wait(K.pe)
                        lastAV = K.pe.matmul(psO[0:64, :], v_tm[:, gc * 128 + hh * 64:gc * 128 + hh * 64 + 64],
                                             PT[:, :], start=(kc == 0), stop=(kc == kcn - 1))
                    K.sem["pe"].inc(lastAV)
                    K.sem["pe"].wait(K.dve)
                    inst = K.dve.tensor_copy(og[hsl, q0:q0 + 128], psO[0:64, :])
                    K.sem["dve"].inc(inst)
                    pp_done_real(K, ppO, inst)
        K.sem["dve"].wait(K.sp)
        K.store(K.scr["s_og"][pr * 128:(pr + 1) * 128, :], og[:, :], eng=K.sp)


def emit_l0ffn_ln(K, l):
    src = K.scr["s_y"]
    gb = K.sb("ffE_gb", [128, 16], F32)
    rowgb = K.sb("ffE_rowgb", [8, 128], F32)
    ysb = K.sb("ffE_y", [128, DC * TT], F32)
    bfsb = K.sb("ffE_bf", [128, DC * TT], BF16)
    sqsb = K.sb("ffE_sq", [128, DC * TT], BF16)
    xg = K.sb("ffE_xg", [128, DC * TT], BF16)
    rows = K.sb("ffE_rows", [1, 6 * TT], F32)
    pp_stat = K.pp("ffE_ps", [1, TT], n=4)
    pp_bc = K.pp("ffE_pb", [128, TT], n=4)
    onesb = K.consts["ones_bf_col"]
    ones_f = K.consts["ones_f_row"]
    for i, nm in enumerate(("lng", "lnb")):
        K.sem["pe"].wait(K.sp)
        K.load(rowgb[:, :], K.inp[f"w_ff{l}_{nm}"])
        K.wait_loads(K.pe)
        K.sem["dve"].wait(K.pe)
        ps = pp_bc.take()
        inst = K.pe.transpose(ps[:, 0:8], rowgb[0:8, :], K.consts["ident_f"][0:8, 0:8])
        K.sem["pe"].inc(inst)
        K.sem["pe"].wait(K.dve)
        inst = K.dve.tensor_copy(gb[:, i * 8:(i + 1) * 8], ps[:, 0:8])
        K.sem["dve"].inc(inst)
        pp_done_real(K, pp_bc, inst)
    if DEV_DBGPAIR:
        K.sem["dve"].wait(K.sp)
        K.store(K.scr["s_x"][0:128, 0:16], gb[:, :], eng=K.sp)
        K.store(K.scr["s_x"][0:8, 32:160], rowgb[:, :], eng=K.sp)
    for t in range(NT):
        t0 = t * TT
        K.sem["dve"].wait(K.sp)
        K.sem["pe"].wait(K.sp)
        K.wait_stores(K.sp)
        half = DC // 2
        K.load(ysb.ap().rearrange("p (c t) -> p c t", c=DC)[:, 0:half, :],
               K.tile_ap(src, t0, TT)[:, 0:half, :])
        if t == 0:
            K.sem["pe"].wait(K.dve)
            K.sem["act"].wait(K.dve)
            for c in range(half, DC):
                K.sem["dve"].inc(K.dve.memset(ysb[:, c * TT:c * TT + 1], 0.0))
                K.load(ysb[:, c * TT + 1:(c + 1) * TT], src[c * 128:(c + 1) * 128, 0:TT - 1])
        else:
            K.load(ysb.ap().rearrange("p (c t) -> p c t", c=DC)[:, half:DC, :],
                   K.tile_ap(src, t0 - 1, TT)[:, half:DC, :])
        K.wait_loads(K.dve)
        K.wait_loads(K.act)
        K.wait_loads(K.pe)
        K.sem["pe"].wait(K.dve)
        K.sem["pe"].wait(K.act)
        K.sem["dve"].wait(K.act)
        K.sem["dve"].inc(K.dve.tensor_copy(bfsb[:, :], ysb[:, :]))
        K.sem["act"].inc(K.act.activation(sqsb[:, :], ysb[:, :], AF.Square))
        K.sem["dve"].wait(K.pe)
        K.sem["act"].wait(K.pe)
        ps_sum = pp_stat.take()
        for c in range(DC):
            K.pe.matmul(ps_sum[:, :], onesb[:, :], bfsb[:, c * TT:(c + 1) * TT],
                        start=(c == 0), stop=(c == DC - 1))
        ps_sq = pp_stat.take()
        last = None
        for c in range(DC):
            last = K.pe.matmul(ps_sq[:, :], onesb[:, :], sqsb[:, c * TT:(c + 1) * TT],
                               start=(c == 0), stop=(c == DC - 1))
        K.sem["pe"].inc(last)
        K.sem["pe"].wait(K.dve)
        mean = rows[0:1, 0:TT]
        var = rows[0:1, TT:2 * TT]
        rstd = rows[0:1, 2 * TT:3 * TT]
        negmean = rows[0:1, 3 * TT:4 * TT]
        m2 = rows[0:1, 4 * TT:5 * TT]
        lnv = rows[0:1, 5 * TT:6 * TT]
        pp_done_real(K, pp_stat, K.dve.tensor_scalar(mean, ps_sum[:, :], 1.0 / D, None, ALU.mult))
        inst = K.dve.tensor_scalar(var, ps_sq[:, :], 1.0 / D, None, ALU.mult)
        K.sem["dve"].inc(inst)
        pp_done_real(K, pp_stat, inst)
        K.sem["dve"].wait(K.dve)
        K.sem["dve"].inc(K.dve.tensor_mul(m2, mean, mean))
        K.sem["dve"].wait(K.dve)
        K.sem["dve"].inc(K.dve.tensor_sub(var, var, m2))
        K.sem["dve"].wait(K.dve)
        K.sem["dve"].inc(K.dve.tensor_scalar(negmean, mean, -1.0, None, ALU.mult))
        K.sem["dve"].wait(K.act)
        K.sem["act"].inc(K.act.activation(lnv, var, AF.Ln, bias=1e-5))
        K.sem["act"].wait(K.act)
        K.sem["act"].inc(K.act.activation(rstd, lnv, AF.Exp, scale=-0.5))
        K.sem["act"].wait(K.pe)
        negmB = pp_bc.take()
        K.pe.matmul(negmB[:, :], ones_f[:, :], negmean, start=True, stop=True)
        rstdB = pp_bc.take()
        K.sem["pe"].inc(K.pe.matmul(rstdB[:, :], ones_f[:, :], rstd, start=True, stop=True))
        K.sem["pe"].wait(K.dve)
        last = None
        for c in range(DC):
            cs = slice(c * TT, (c + 1) * TT)
            K.dve.tensor_add(ysb[:, cs], ysb[:, cs], negmB[:, :])
            K.dve.tensor_mul(ysb[:, cs], ysb[:, cs], rstdB[:, :])
            last = K.dve.tensor_scalar(xg[:, cs], ysb[:, cs], gb[:, c:c + 1],
                                       gb[:, 8 + c:8 + c + 1], ALU.mult, ALU.add)
        K.sem["dve"].inc(last)
        K.bump("dve", pp_bc.sem, 2)
        pp_bc.dones += 2
        K.sem["dve"].wait(K.sp)
        K.store(K.tile_ap(K.scr["s_xg"], t0, TT),
                xg.ap().rearrange("p (c t) -> p c t", c=DC), eng=K.sp)


def emit_l0ffn_glu(K, l):
    HPC = 22
    w_dram = K.inp[f"w_ff{l}_w1"]
    N = 5632
    wsb = K.sb("ffF_w", [128, DC * N], BF16)
    for kc in range(DC):
        K.load(wsb[:, kc * N:(kc + 1) * N], w_dram[kc * 128:(kc + 1) * 128, :], cast=True)
    brow = K.sb("ffF_bias", [1, N], BF16)
    K.load(brow[:, :], K.inp[f"w_ff{l}_b1"], cast=True)
    xsb = K.sb("ffF_x", [128, DC * TT], BF16)
    osb = K.sb("ffF_o", [128, HPC * TT], BF16)
    tmp2 = K.sb("ffF_t2", [128, TT], F32)
    pp = K.pp("ffF_pp", [128, TT], n=4)
    onesr = K.consts["ones_bf_row"]
    for t in range(NT):
        t0 = t * TT
        K.sem["pe"].wait(K.sp)
        K.wait_stores(K.sp)
        K.load(xsb.ap().rearrange("p (c t) -> p c t", c=DC),
               K.tile_ap(K.scr["s_xg"], t0, TT))
        K.wait_loads(K.pe)
        for j in range(HPC):
            psA = pp.take()
            K.pe.matmul(psA[:, :], brow[0:1, j * 128:(j + 1) * 128], onesr[:, :],
                        start=True, stop=False)
            for kc in range(DC):
                K.pe.matmul(psA[:, :], wsb[:, kc * N + j * 128:kc * N + (j + 1) * 128],
                            xsb[:, kc * TT:(kc + 1) * TT], start=False, stop=(kc == DC - 1))
            psB = pp.take()
            K.pe.matmul(psB[:, :], brow[0:1, 2816 + j * 128:2816 + (j + 1) * 128], onesr[:, :],
                        start=True, stop=False)
            last = None
            for kc in range(DC):
                last = K.pe.matmul(psB[:, :], wsb[:, kc * N + 2816 + j * 128:kc * N + 2816 + (j + 1) * 128],
                                   xsb[:, kc * TT:(kc + 1) * TT], start=False, stop=(kc == DC - 1))
            K.sem["pe"].inc(last)
            K.sem["pe"].wait(K.act)
            K.sem["act"].inc(K.act.activation(tmp2[:, :], psB[:, :], AF.Gelu))
            K.sem["act"].wait(K.dve)
            inst = K.dve.tensor_mul(osb[:, j * TT:(j + 1) * TT], psA[:, :], tmp2[:, :])
            K.sem["dve"].inc(inst)
            pp_done_real(K, pp, inst, k=2)
        K.sem["dve"].wait(K.sp)
        K.store(K.tile_ap(K.scr["s_glu"], t0, TT, chunks=HPC),
                osb.ap().rearrange("p (c t) -> p c t", c=HPC), eng=K.sp)




def _l1_slopes():
    return [2.0 ** (-0.5 * (i + 1)) for i in range(16)]


def emit_l1attn(K):
    old_es = K.phase_es
    with ExitStack() as es:
        K.phase_es = es
        osb_holder = {}

        def evac(ncx, nw, t, ps, osb, pp):
            osb_holder["osb"] = osb
            cs = slice(ncx * TT, (ncx + 1) * TT)
            if ncx < 8:
                inst = K.dve.tensor_scalar(osb[:, cs], ps[:, :], 0.125, None, ALU.mult)
            else:
                inst = K.dve.tensor_copy(osb[:, cs], ps[:, :])
            K.sem["dve"].inc(inst)
            pp_done_real(K, pp, inst)

        def store_cb(t, osb):
            t0 = t * TT
            K.sem["dve"].wait(K.sp)
            for i, dst in enumerate(("s_q", "s_k", "s_v")):
                K.store(K.tile_ap(K.scr[dst], t0, TT),
                        osb[:, i * DC * TT:(i + 1) * DC * TT].rearrange("p (c t) -> p c t", c=DC),
                        eng=K.sp)

        emit_proj(K, "l1A", K.inp["w_l1a_wqkv"], D, 3072, K.scr["s_xnb"],
                  bias_row_dram=K.inp["w_l1a_bqkv"], evac=evac, store_cb=store_cb)
        K.phase_es = old_es
    K.barrier()
    with ExitStack() as es:
        K.phase_es = es
        emit_l1attn_core(K)
        K.phase_es = old_es
    K.barrier()
    with ExitStack() as es:
        K.phase_es = es
        emit_proj(K, "l1C", K.inp["w_l1a_wout"], D, D, K.scr["s_o"],
                  out_dram=K.scr["s_a"], out_dtype=F32,
                  bias_row_dram=K.inp["w_l1a_bout"])
        K.phase_es = old_es


def emit_l1attn_core(K):
    slopes = _l1_slopes()
    posB = K.sb("l1B_posB", [128, L], F32)
    q_h = K.sb("l1B_qh", [64, L], BF16)
    k_h = K.sb("l1B_kh", [64, L], BF16)
    vp = K.sb("l1B_v", [128, L], BF16)
    v_tm = K.sb("l1B_vtm", [128, L], BF16)
    stash = K.sb("l1B_stash", [128, L], F32)
    P_sb = K.sb("l1B_P", [128, L], BF16)
    Dt = K.sb("l1B_D", [128, 128], BF16)
    PT = K.sb("l1B_PT", [128, 128], BF16)
    negm = K.sb("l1B_negm", [128, 1], F32)
    rs = K.sb("l1B_rs", [128, 2], F32)
    o_sb = K.sb("l1B_o", [128, L], BF16)
    ppS = K.pp("l1B_ppS", [128, TT], n=2)
    ppT = K.pp("l1B_ppT", [128, 128], n=2, dtype=BF16)
    ppD = K.pp("l1B_ppD", [128, 128], n=2)
    ppO = K.pp("l1B_ppO", [64, 128], n=2)
    identb = K.consts["ident_bf"]
    ppf = ppS
    for t in range(NT):
        ps = ppf.take()
        inst = K.pe.matmul(ps[:, :], K.consts["ones_f_row"][:, :],
                           K.consts["pos"][0:1, t * TT:(t + 1) * TT], start=True, stop=True)
        K.sem["pe"].inc(inst)
        K.sem["pe"].wait(K.dve)
        inst = K.dve.tensor_copy(posB[:, t * TT:(t + 1) * TT], ps[:, :])
        K.sem["dve"].inc(inst)
        pp_done_real(K, ppf, inst)
    for pr in range(8):
        K.sem["pe"].wait(K.sp)
        K.sem["dve"].wait(K.sp)
        K.sem["act"].wait(K.sp)
        K.wait_stores(K.sp)
        K.load(vp[:, :], K.scr["s_v"][pr * 128:(pr + 1) * 128, :])
        K.wait_loads(K.pe)
        for c in range(32):
            ps = ppT.take()
            inst = K.pe.transpose(ps[:, :], vp[:, c * 128:(c + 1) * 128], identb[:, :])
            K.sem["pe"].inc(inst)
            K.sem["pe"].wait(K.dve)
            inst = K.dve.tensor_copy(v_tm[:, c * 128:(c + 1) * 128], ps[:, :])
            K.sem["dve"].inc(inst)
            pp_done_real(K, ppT, inst)
        for hh in range(2):
            h = 2 * pr + hh
            slope = slopes[h]
            K.sem["pe"].wait(K.sp)
            K.load(q_h[:, :], K.scr["s_q"][h * 64:(h + 1) * 64, :])
            K.load(k_h[:, :], K.scr["s_k"][h * 64:(h + 1) * 64, :])
            K.wait_loads(K.pe)
            for qt in range(32):
                nkt = qt // 4 + 1
                Wt = 128 * qt + 128
                for kt in range(nkt):
                    w = 512 if kt < nkt - 1 else (qt % 4) * 128 + 128
                    c0 = kt * 512
                    psS = ppS.take()
                    inst = K.pe.matmul(psS[:, 0:w], q_h[:, qt * 128:(qt + 1) * 128],
                                       k_h[:, c0:c0 + w], start=True, stop=True)
                    K.sem["pe"].inc(inst)
                    K.sem["pe"].wait(K.dve)
                    inst = K.dve.scalar_tensor_tensor(stash[:, c0:c0 + w], posB[:, c0:c0 + w],
                                                      slope, psS[:, 0:w], ALU.mult, ALU.add)
                    K.sem["dve"].inc(inst)
                    pp_done_real(K, ppS, inst)
                K.sem["dve"].wait(K.dve)
                K.sem["dve"].inc(K.dve.tensor_add(stash[:, Wt - 128:Wt], stash[:, Wt - 128:Wt],
                                                  K.consts["mask_leq"][:, :]))
                K.sem["dve"].wait(K.dve)
                K.sem["dve"].inc(K.dve.reduce_max(negm[:, :], stash[:, 0:Wt], axis=AX.X,
                                                  negate=True))
                K.sem["dve"].wait(K.act)
                K.sem["act"].inc(K.act.activation(P_sb[:, 0:Wt], stash[:, 0:Wt], AF.Exp,
                                                  bias=negm[:, :], accum_out=rs[:, 0:1]))
                K.sem["act"].wait(K.dve)
                K.sem["dve"].inc(K.dve.reciprocal(rs[:, 1:2], rs[:, 0:1]))
                K.sem["dve"].wait(K.dve)
                K.sem["dve"].inc(K.dve.tensor_scalar(Dt[:, :], identb[:, :], rs[:, 1:2],
                                                     None, ALU.mult))
                K.sem["dve"].wait(K.pe)
                psO = ppO.take()
                lastAV = None
                for c in range(qt + 1):
                    psD = ppD.take()
                    inst = K.pe.matmul(psD[:, :], P_sb[:, c * 128:(c + 1) * 128], Dt[:, :],
                                       start=True, stop=True)
                    K.sem["pe"].inc(inst)
                    K.sem["pe"].wait(K.dve)
                    inst = K.dve.tensor_copy(PT[:, :], psD[:, :])
                    K.sem["dve"].inc(inst)
                    pp_done_real(K, ppD, inst)
                    K.sem["dve"].wait(K.pe)
                    lastAV = K.pe.matmul(psO[0:64, :],
                                         v_tm[:, c * 128 + hh * 64:c * 128 + hh * 64 + 64],
                                         PT[:, :], start=(c == 0), stop=(c == qt))
                K.sem["pe"].inc(lastAV)
                K.sem["pe"].wait(K.dve)
                inst = K.dve.tensor_copy(o_sb[hh * 64:(hh + 1) * 64, qt * 128:(qt + 1) * 128],
                                         psO[0:64, :])
                K.sem["dve"].inc(inst)
                pp_done_real(K, ppO, inst)
        K.sem["dve"].wait(K.sp)
        K.store(K.scr["s_o"][pr * 128:(pr + 1) * 128, :], o_sb[:, :], eng=K.sp)




def emit_moe(K):
    old_es = K.phase_es
    comb_tm = K.sb("moe_comb", [128, 32 * 8], F32)
    with ExitStack() as es:
        K.phase_es = es
        emit_moe_router(K, comb_tm)
        K.phase_es = old_es
    if DEV_DBGPAIR:
        K.sem["dve"].wait(K.sp)
        K.store(K.scr["s_y"][0:128, 0:256], comb_tm[:, :], eng=K.sp)
    K.barrier()
    for p in range(DEV_MOE_PASSES):
        e = p - 1
        with ExitStack() as es:
            K.phase_es = es
            if p == 0:
                w1 = K.inp["w_moe_sw1"]
                b1 = K.inp["w_moe_sb1"]
            else:
                w1 = K.inp["w_moe_ew1"][e]
                b1 = K.inp["w_moe_eb1"][e:e + 1, :]

            def gelu_evac(ncx, nw, t, ps, osb, pp):
                K.sem["pe"].wait(K.act)
                cs = slice(ncx * TT, (ncx + 1) * TT)
                inst = K.act.activation(osb[:, cs], ps[:, :], AF.Gelu_apprx_tanh)
                K.sem["act"].inc(inst)
                pp_done_real(K, pp, inst, eng="act")

            emit_proj(K, f"moeA{p}", w1, D, 4096, K.scr["s_xnb"],
                      bias_row_dram=b1, evac=gelu_evac, out_dram=K.scr["s_g"])
            K.phase_es = old_es
        K.barrier()
        with ExitStack() as es:
            K.phase_es = es
            emit_moe_b(K, p, comb_tm)
            K.phase_es = old_es
        K.barrier()


def emit_moe_router(K, comb_tm):
    wr = K.sb("moeR_w", [128, 64], F32)
    br = K.sb("moeR_br", [1, 8], F32)
    xf = K.sb("moeR_x", [128, DC * 128], F32)
    r0 = K.sb("moeR_r", [128, 8 * 6], F32)
    cvec = K.sb("moeR_c", [128, 4], F32)
    pp = K.pp("moeR_pp", [128, TT], n=2)
    for kc in range(DC):
        K.load(wr[:, kc * 8:(kc + 1) * 8], K.inp["w_moe_rw"][kc * 128:(kc + 1) * 128, :])
    K.load(br[:, :], K.inp["w_moe_rb"])
    for c in range(32):
        t0 = c * 128
        K.sem["pe"].wait(K.sp)
        K.load(xf.ap().rearrange("p (k t) -> p k t", k=DC),
               K.tile_ap(K.scr["s_xn"], t0, 128, chunks=DC))
        K.wait_loads(K.pe)
        ps = pp.take()
        K.pe.matmul(ps[:, 0:8], K.consts["ones_f_row"][:, :], br[:, :], start=True, stop=False)
        last = None
        for kc in range(DC):
            last = K.pe.matmul(ps[:, 0:8], xf[:, kc * 128:(kc + 1) * 128], wr[:, kc * 8:(kc + 1) * 8],
                               start=False, stop=(kc == DC - 1))
        K.sem["pe"].inc(last)
        K.sem["pe"].wait(K.dve)
        K.sem["dve"].inc(K.dve.tensor_copy(r0[:, 0:8], ps[:, 0:8]))
        K.sem["dve"].wait(K.dve)
        inst = K.dve.reduce_max(cvec[:, 0:1], r0[:, 0:8], axis=AX.X, negate=True)
        K.sem["dve"].inc(inst)
        pp_done_real(K, pp, inst)
        K.sem["dve"].wait(K.act)
        K.sem["act"].inc(K.act.activation(r0[:, 8:16], r0[:, 0:8], AF.Exp, bias=cvec[:, 0:1],
                                          accum_out=cvec[:, 1:2]))
        K.sem["act"].wait(K.dve)
        K.sem["dve"].inc(K.dve.reciprocal(cvec[:, 2:3], cvec[:, 1:2]))
        K.sem["dve"].wait(K.dve)
        K.sem["dve"].inc(K.dve.tensor_scalar(r0[:, 0:8], r0[:, 8:16], cvec[:, 2:3], None, ALU.mult))
        K.sem["dve"].wait(K.dve)
        K.sem["dve"].inc(K.dve.reduce_max(cvec[:, 0:1], r0[:, 0:8], axis=AX.X))
        K.sem["dve"].wait(K.dve)
        K.sem["dve"].inc(K.dve.tensor_scalar(r0[:, 24:32], r0[:, 0:8], cvec[:, 0:1], None, ALU.is_ge))
        K.sem["dve"].wait(K.dve)
        K.dve.tensor_mul(r0[:, 32:40], r0[:, 24:32], r0[:, 0:8])
        K.sem["dve"].inc(K.dve.tensor_sub(r0[:, 40:48], r0[:, 0:8], r0[:, 24:32]))
        K.sem["dve"].wait(K.dve)
        K.sem["dve"].inc(K.dve.tensor_sub(r0[:, 40:48], r0[:, 40:48], r0[:, 32:40]))
        K.sem["dve"].wait(K.dve)
        K.sem["dve"].inc(K.dve.reduce_max(cvec[:, 3:4], r0[:, 40:48], axis=AX.X))
        K.sem["dve"].wait(K.dve)
        K.sem["dve"].inc(K.dve.tensor_scalar(r0[:, 24:32], r0[:, 0:8], cvec[:, 3:4], None, ALU.is_ge))
        K.sem["dve"].wait(K.dve)
        K.sem["dve"].inc(K.dve.tensor_mul(r0[:, 16:24], r0[:, 0:8], r0[:, 24:32]))
        K.sem["dve"].wait(K.dve)
        K.sem["dve"].inc(K.dve.tensor_reduce(cvec[:, 1:2], r0[:, 16:24], axis=AX.X, op=ALU.add))
        K.sem["dve"].wait(K.dve)
        K.sem["dve"].inc(K.dve.reciprocal(cvec[:, 2:3], cvec[:, 1:2]))
        K.sem["dve"].wait(K.dve)
        K.sem["dve"].inc(K.dve.tensor_scalar(comb_tm[:, c * 8:(c + 1) * 8], r0[:, 16:24],
                                             cvec[:, 2:3], None, ALU.mult))
        K.sem["dve"].wait(K.dve)


def emit_moe_b(K, p, comb_tm):
    e = p - 1
    if p == 0:
        w2_dram = K.inp["w_moe_sw2"]
        b2 = K.inp["w_moe_sb2"]
    else:
        w2_dram = K.inp["w_moe_ew2"][e]
        b2 = K.inp["w_moe_eb2"][e:e + 1, :]
    KC = 32
    wsb = K.sb("moeB_w", [128, KC * D], BF16)
    for kc in range(KC):
        K.load(wsb[:, kc * D:(kc + 1) * D], w2_dram[kc * 128:(kc + 1) * 128, :], cast=True)
    brow = K.sb("moeB_bias", [1, D], BF16)
    K.load(brow[:, :], b2, cast=True)
    xsb = K.sb("moeB_x", [128, KC * TT], BF16)
    ysb = K.sb("moeB_y", [128, DC * TT], F32)
    tmp = K.sb("moeB_tmp", [128, TT], F32)
    combB = None
    if p > 0:
        combB = K.sb("moeB_cB", [128, L], F32)
        ppc = K.pp("moeB_ppc", [128, 128], n=2)
        De = K.sb("moeB_De", [128, 128], F32)
        for c in range(32):
            K.sem["pe"].wait(K.dve)
            inst = K.dve.tensor_scalar(De[:, :], K.consts["ident_f"][:, :],
                                       comb_tm[:, c * 8 + e:c * 8 + e + 1], None, ALU.mult)
            K.sem["dve"].inc(inst)
            K.sem["dve"].wait(K.pe)
            ps = ppc.take()
            inst = K.pe.matmul(ps[:, :], K.consts["ones128_f"][:, :], De[:, :],
                               start=True, stop=True)
            K.sem["pe"].inc(inst)
            K.sem["pe"].wait(K.dve)
            inst = K.dve.tensor_copy(combB[:, c * 128:(c + 1) * 128], ps[:, :])
            K.sem["dve"].inc(inst)
            pp_done_real(K, ppc, inst)
    pp = K.pp("moeB_pp", [128, TT], n=4)
    onesr = K.consts["ones_bf_row"]
    for t in range(NT):
        t0 = t * TT
        K.sem["pe"].wait(K.sp)
        K.sem["dve"].wait(K.sp)
        K.wait_stores(K.sp)
        K.load(xsb.ap().rearrange("p (c t) -> p c t", c=KC),
               K.tile_ap(K.scr["s_g"], t0, TT, chunks=KC))
        K.load(ysb.ap().rearrange("p (c t) -> p c t", c=DC),
               K.tile_ap(K.scr["s_a"], t0, TT, chunks=DC))
        K.wait_loads(K.pe)
        K.wait_loads(K.dve)
        for oc in range(DC):
            ps = pp.take()
            K.pe.matmul(ps[:, :], brow[0:1, oc * 128:(oc + 1) * 128], onesr[:, :],
                        start=True, stop=False)
            last = None
            for kc in range(KC):
                last = K.pe.matmul(ps[:, :], wsb[:, kc * D + oc * 128:kc * D + (oc + 1) * 128],
                                   xsb[:, kc * TT:(kc + 1) * TT], start=False, stop=(kc == KC - 1))
            K.sem["pe"].inc(last)
            K.sem["pe"].wait(K.dve)
            cs = slice(oc * TT, (oc + 1) * TT)
            if p == 0:
                inst = K.dve.tensor_copy(ysb[:, cs], ps[:, :])
                K.sem["dve"].inc(inst)
                pp_done_real(K, pp, inst)
            else:
                inst = K.dve.tensor_mul(tmp[:, :], ps[:, :], combB[:, t0:t0 + TT])
                K.sem["dve"].inc(inst)
                pp_done_real(K, pp, inst)
                K.sem["dve"].wait(K.dve)
                K.sem["dve"].inc(K.dve.tensor_add(ysb[:, cs], ysb[:, cs], tmp[:, :]))
        K.sem["dve"].wait(K.sp)
        K.store(K.tile_ap(K.scr["s_a"], t0, TT, chunks=DC),
                ysb.ap().rearrange("p (c t) -> p c t", c=DC), eng=K.sp)


def emit_p0(K):
    emit_sandwich_ln(K, K.inp["x_fm"], None, None, K.scr["s_xnb"], pass_b_only=True)


def emit_ln1(K):
    emit_sandwich_ln(K, K.scr["s_a"], K.inp["x_fm"], K.scr["s_x"], None,
                     xn_f32_dram=K.scr["s_xn"])


def emit_ln2(K):
    emit_sandwich_ln(K, K.scr["s_a"], K.scr["s_x"], K.scr["s_x2"], K.scr["s_xnb"])


def emit_ln3(K):
    emit_sandwich_ln(K, K.scr["s_a"], K.scr["s_x2"], K.scr["s_x"], K.scr["s_xnb"],
                     xn_f32_dram=K.scr["s_xn"])


def emit_ln4(K):
    emit_sandwich_ln(K, K.scr["s_a"], K.scr["s_x"], K.out["out_fm"], None)


PHASES = [
    ("p0", emit_p0),
    ("l0attn", emit_l0attn),
    ("ln1", emit_ln1),
    ("ff0", lambda K: emit_l0ffn_layer(K, 0)),
    ("ff1", lambda K: emit_l0ffn_layer(K, 1)),
    ("ln2", emit_ln2),
    ("l1attn", emit_l1attn),
    ("ln3", emit_ln3),
    ("moe", emit_moe),
    ("ln4", emit_ln4),
]


def build_program(upto=None, dump=()):
    nc = bass.Bass("TRN2", target_bir_lowering=False)
    es = ExitStack()
    K = Kctx(nc, es)
    declare_io(nc, K)
    with es:
        init_consts(K)
        K.barrier()
        phases = PHASES if upto is None else PHASES[:upto]
        for name, emit in phases:
            with ExitStack() as pes:
                K.phase_es = pes
                emit(K)
                K.phase_es = es
            K.barrier()
        for name in dump:
            src_ = K.scr[name]
            dbg = nc.dram_tensor("dbg_" + name, list(src_.tensor.shape),
                                 src_.tensor.dtype, kind="ExternalOutput").ap()
            K.store(dbg, src_, eng=K.sp)
        K.barrier()
    return nc



def kernel(**inputs):
    shared = prep_shared(inputs)
    x = np.asarray(inputs["L0_attn_x"], dtype=np.float32)
    nc = build_program(upto=DEV_UPTO, dump=DEV_DUMP)
    in_maps = []
    for b in range(8):
        m = dict(shared)
        m["x_fm"] = np.ascontiguousarray(x[b].T)
        in_maps.append(m)
    res = run_bass_kernel_spmd(nc, in_maps, core_ids=list(range(8)))
    out = np.empty((8, L, D), np.float32)
    for b in range(8):
        out[b] = res.results[b]["out_fm"].T
    if DEV_DUMP:
        kernel.dev_results = res.results
    return out


# version 02 (final)
# speedup vs baseline: 1.0032x; 1.0032x over previous
import numpy as np
from contextlib import ExitStack
import concourse.bass as bass
import concourse.mybir as mybir
from concourse.bass_utils import run_bass_kernel_spmd

F32 = mybir.dt.float32
BF16 = mybir.dt.bfloat16
AF = mybir.ActivationFunctionType
ALU = mybir.AluOpType
AX = mybir.AxisListType

L = 4096
D = 1024
TT = 512
NT = L // TT
DC = D // 128
NEG = -1.0e30

DEV_UPTO = None
DEV_DUMP = ()
DEV_STAGE = 99

DEV_TILES = None
DEV_DBGPAIR = False
DEV_MOE_PASSES = 9



def _rotary_maps():
    DIM_HEAD, WINDOW, n = 64, 512, L
    half_idx = np.arange(0, DIM_HEAD, 2, dtype=np.float32)
    inv_freq = 1.0 / 10000.0 ** (half_idx.astype(np.float64) / DIM_HEAD)
    t = np.arange(n, dtype=np.float64)
    freqs = t[:, None] * inv_freq[None, :]
    freqs = np.concatenate([freqs, freqs], axis=-1)
    base = (half_idx.astype(np.float64) + 0.4 * DIM_HEAD) / (1.4 * DIM_HEAD)
    power = (t - n // 2) / float(WINDOW // 2)
    scale = base[None, :] ** power[:, None]
    scale = np.concatenate([scale, scale], axis=-1)
    cos, sin = np.cos(freqs), np.sin(freqs)
    CQ = (cos * scale).T.astype(np.float32)
    SQ = (sin * scale).T.astype(np.float32)
    CK = (cos / scale).T.astype(np.float32)
    SK = (sin / scale).T.astype(np.float32)
    return CQ, SQ, CK, SK


def _host_consts():
    c = {}
    c["c_ident"] = np.eye(128, dtype=np.float32)
    r = np.arange(128).reshape(128, 1)
    cc = np.arange(128).reshape(1, 128)
    c["c_mask_geq"] = np.where(cc >= r, 0.0, NEG).astype(np.float32)
    c["c_mask_leq"] = np.where(cc <= r, 0.0, NEG).astype(np.float32)
    c["c_pos"] = np.arange(L, dtype=np.float32).reshape(1, L)
    CQ, SQ, CK, SK = _rotary_maps()
    c["c_cq"], c["c_sq"], c["c_ck"], c["c_sk"] = CQ, SQ, CK, SK
    R = np.zeros((128, 128), np.float32)
    for blk in range(2):
        o = 64 * blk
        for dd in range(32):
            R[o + dd + 32, o + dd] = -1.0
            R[o + dd, o + dd + 32] = 1.0
    c["c_r128"] = R
    return c


def prep_shared(inputs):
    f = lambda a: np.ascontiguousarray(np.asarray(a, dtype=np.float32))
    g = {}
    for n in ("Wq", "Wk", "Wv", "Wo", "Wb", "Wfd", "Wsd", "Wtf1"):
        g["w_l0a_" + n] = f(inputs["L0_attn_" + n])
    g["w_l0a_bfd"] = f(inputs["L0_attn_bfd"]).reshape(1, 16)
    g["w_l0a_bsd"] = f(inputs["L0_attn_bsd"]).reshape(1, 16)
    g["w_l0a_btf1"] = f(inputs["L0_attn_btf1"]).reshape(1, 32)
    g["w_l0a_wtf2"] = f(inputs["L0_attn_Wtf2"]).reshape(1, 32)
    g["w_l0a_btf2"] = f(inputs["L0_attn_btf2"]).reshape(1, 1)
    w1 = f(inputs["L0_ffn_w1"])
    b1 = f(inputs["L0_ffn_b1"])
    w2 = f(inputs["L0_ffn_w2"])
    HID = 2730
    HP = 2816
    w1p = np.zeros((2, D, 2 * HP), np.float32)
    b1p = np.zeros((2, 1, 2 * HP), np.float32)
    w1p[:, :, :HID] = w1[:, :, :HID]
    w1p[:, :, HP:HP + HID] = w1[:, :, HID:]
    b1p[:, 0, :HID] = b1[:, :HID]
    b1p[:, 0, HP:HP + HID] = b1[:, HID:]
    w2p = np.zeros((2, HP, D), np.float32)
    w2p[:, :HID, :] = w2
    perm = np.concatenate([np.arange(32, 64), np.arange(0, 32)])
    for ll in range(2):
        g[f"w_ff{ll}_wqkv"] = f(inputs["L0_ffn_wqkv"][ll])
        g[f"w_ff{ll}_qs"] = f(inputs["L0_ffn_q_scale"][ll]).reshape(1, 64)
        g[f"w_ff{ll}_qsp"] = f(inputs["L0_ffn_q_scale"][ll])[perm].reshape(1, 64)
        g[f"w_ff{ll}_ks"] = f(inputs["L0_ffn_k_scale"][ll]).reshape(1, 64)
        g[f"w_ff{ll}_ksp"] = f(inputs["L0_ffn_k_scale"][ll])[perm].reshape(1, 64)
        g[f"w_ff{ll}_wo"] = f(inputs["L0_ffn_wo"][ll])
        g[f"w_ff{ll}_lng"] = f(inputs["L0_ffn_ln_g"][ll]).reshape(8, 128)
        g[f"w_ff{ll}_lnb"] = f(inputs["L0_ffn_ln_b"][ll]).reshape(8, 128)
        g[f"w_ff{ll}_w1"] = np.ascontiguousarray(w1p[ll])
        g[f"w_ff{ll}_b1"] = np.ascontiguousarray(b1p[ll])
        g[f"w_ff{ll}_w2"] = np.ascontiguousarray(w2p[ll])
        g[f"w_ff{ll}_b2"] = f(inputs["L0_ffn_b2"][ll]).reshape(1, D)
    g["w_l1a_wqkv"] = f(inputs["L1_attn_W_qkv"])
    g["w_l1a_bqkv"] = f(inputs["L1_attn_b_qkv"]).reshape(1, 3072)
    g["w_l1a_wout"] = f(inputs["L1_attn_W_out"])
    g["w_l1a_bout"] = f(inputs["L1_attn_b_out"]).reshape(1, D)
    g["w_moe_sw1"] = np.ascontiguousarray(f(inputs["L1_ffn_shared_w1"]).T)
    g["w_moe_sb1"] = f(inputs["L1_ffn_shared_b1"]).reshape(1, 4096)
    g["w_moe_sw2"] = np.ascontiguousarray(f(inputs["L1_ffn_shared_w2"]).T)
    g["w_moe_sb2"] = f(inputs["L1_ffn_shared_b2"]).reshape(1, D)
    g["w_moe_rw"] = np.ascontiguousarray(f(inputs["L1_ffn_router_w"]).T)
    g["w_moe_rb"] = f(inputs["L1_ffn_router_b"]).reshape(1, 8)
    g["w_moe_ew1"] = np.ascontiguousarray(f(inputs["L1_ffn_spec_w1"]).transpose(0, 2, 1))
    g["w_moe_eb1"] = f(inputs["L1_ffn_spec_b1"])
    g["w_moe_ew2"] = np.ascontiguousarray(f(inputs["L1_ffn_spec_w2"]).transpose(0, 2, 1))
    g["w_moe_eb2"] = f(inputs["L1_ffn_spec_b2"])
    g.update(_host_consts())
    return g



class Sem:
    __slots__ = ("h", "n")

    def __init__(self, h):
        self.h = h
        self.n = 0

    def inc(self, inst, k=1):
        inst.then_inc(self.h, k)
        self.n += k

    def wait(self, eng, val=None):
        eng.wait_ge(self.h, self.n if val is None else val)


class PPool:

    def __init__(self, K, name, shape, n=2, dtype=F32):
        self.K = K
        self.n = n
        self.tiles = [K.es.enter_context(K.nc.psum_tensor(f"{name}{i}", list(shape), dtype))
                      for i in range(n)]
        self.sem = Sem(K.es.enter_context(K.nc.semaphore(f"sem_{name}")))
        self.uses = 0
        self.dones = 0

    def take(self):
        b = self.uses % self.n
        need = self.uses - self.n + 1
        self.uses += 1
        if need > 0:
            self.K.pe.wait_ge(self.sem.h, need)
        return self.tiles[b]

    def done(self, inst, k=1):
        self.sem.inc(inst, k)
        self.dones += k


class Kctx:
    def __init__(self, nc, es):
        self.nc = nc
        self.es = es
        self.pe = nc.tensor
        self.dve = nc.vector
        self.act = nc.scalar
        self.pool = nc.gpsimd
        self.sp = nc.sync
        self.engines = {"pe": self.pe, "dve": self.dve, "act": self.act,
                        "pool": self.pool, "sp": self.sp}
        self.sem = {}
        for name in ("ld", "ldg", "st", "stg", "pe", "dve", "act", "pool", "sp"):
            self.sem[name] = Sem(es.enter_context(nc.semaphore("sem_" + name)))
        self.inp = {}
        self._names = {}
        self.scr = {}
        self.out = {}
        self.consts = {}
        self.phase_es = es

    def _uniq(self, name):
        n = self._names.get(name, 0)
        self._names[name] = n + 1
        return name if n == 0 else f"{name}__{n}"

    def sb(self, name, shape, dt, phase=True):
        es = self.phase_es if phase else self.es
        return es.enter_context(self.nc.sbuf_tensor(self._uniq(name), list(shape), dt))

    def pp(self, name, shape, n=2, dtype=F32):
        K2 = self
        class _tmp:
            pass
        name = self._uniq(name)
        p = PPool.__new__(PPool)
        p.K = K2
        p.n = n
        p.tiles = [self.phase_es.enter_context(self.nc.psum_tensor(f"{name}{i}", list(shape), dtype))
                   for i in range(n)]
        p.sem = Sem(self.phase_es.enter_context(self.nc.semaphore(f"sem_{name}")))
        p.uses = 0
        p.dones = 0
        return p

    def load(self, dst, src, cast=False):
        if cast:
            self.sem["ldg"].inc(self.pool.dma_start(out=dst, in_=src), 16)
        else:
            self.sem["ld"].inc(self.sp.dma_start(out=dst, in_=src), 16)

    def store(self, dst, src, eng=None):
        eng = eng or self.pool
        nm = "st" if eng is self.sp else "stg"
        self.sem[nm].inc(eng.dma_start(out=dst, in_=src), 16)

    def wait_stores(self, eng):
        self.sem["st"].wait(eng)
        self.sem["stg"].wait(eng)

    def wait_loads(self, eng):
        self.sem["ld"].wait(eng)
        self.sem["ldg"].wait(eng)

    def tile_ap(self, dram, t0, w, chunks=DC, csz=128):
        return dram[:, t0:t0 + w].rearrange("(c p) t -> p c t", p=csz)

    def bump(self, eng_name, sem, k=1):
        eng = self.engines[eng_name]
        self.sem[eng_name].wait(eng)
        sem.inc(eng.nop(), k)

    def barrier(self):
        for nm2, eng in self.engines.items():
            for nm in ("pe", "dve", "act", "pool", "sp", "ld", "ldg", "st", "stg"):
                if nm != nm2:
                    self.sem[nm].wait(eng)


def declare_io(nc, K, shapes_only=False):
    def din(name, shape, dt=F32):
        K.inp[name] = nc.dram_tensor(name, list(shape), dt, kind="ExternalInput").ap()

    din("x_fm", [D, L])
    din("w_l0a_Wq", [D, D]); din("w_l0a_Wk", [D, D]); din("w_l0a_Wv", [D, D])
    din("w_l0a_Wo", [D, D])
    din("w_l0a_Wb", [D, 16]); din("w_l0a_Wfd", [D, 16]); din("w_l0a_Wsd", [D, 16])
    din("w_l0a_bfd", [1, 16]); din("w_l0a_bsd", [1, 16])
    din("w_l0a_Wtf1", [64, 32]); din("w_l0a_btf1", [1, 32])
    din("w_l0a_wtf2", [1, 32]); din("w_l0a_btf2", [1, 1])
    for ll in range(2):
        din(f"w_ff{ll}_wqkv", [D, 1536])
        din(f"w_ff{ll}_qs", [1, 64]); din(f"w_ff{ll}_qsp", [1, 64])
        din(f"w_ff{ll}_ks", [1, 64]); din(f"w_ff{ll}_ksp", [1, 64])
        din(f"w_ff{ll}_wo", [512, D])
        din(f"w_ff{ll}_lng", [8, 128]); din(f"w_ff{ll}_lnb", [8, 128])
        din(f"w_ff{ll}_w1", [D, 5632]); din(f"w_ff{ll}_b1", [1, 5632])
        din(f"w_ff{ll}_w2", [2816, D]); din(f"w_ff{ll}_b2", [1, D])
    din("w_l1a_wqkv", [D, 3072]); din("w_l1a_bqkv", [1, 3072])
    din("w_l1a_wout", [D, D]); din("w_l1a_bout", [1, D])
    din("w_moe_sw1", [D, 4096]); din("w_moe_sb1", [1, 4096])
    din("w_moe_sw2", [4096, D]); din("w_moe_sb2", [1, D])
    din("w_moe_rw", [D, 8]); din("w_moe_rb", [1, 8])
    din("w_moe_ew1", [8, D, 4096]); din("w_moe_eb1", [8, 4096])
    din("w_moe_ew2", [8, 4096, D]); din("w_moe_eb2", [8, 1024])
    din("c_ident", [128, 128]); din("c_mask_geq", [128, 128]); din("c_mask_leq", [128, 128])
    din("c_pos", [1, L])
    din("c_cq", [64, L]); din("c_sq", [64, L]); din("c_ck", [64, L]); din("c_sk", [64, L])
    din("c_r128", [128, 128])
    def scr(name, shape, dt):
        K.scr[name] = nc.dram_tensor(name, list(shape), dt, kind="Internal").ap()

    scr("s_x", [D, L], F32)
    scr("s_x2", [D, L], F32)
    scr("s_a", [D, L], F32)
    scr("s_xnb", [D, L], BF16)
    scr("s_xn", [D, L], F32)
    scr("s_q", [D, L], BF16)
    scr("s_k", [D, L], BF16)
    scr("s_v", [D, L], BF16)
    scr("s_o", [D, L], BF16)
    scr("s_y", [D, L], F32)
    scr("s_y2", [D, L], F32)
    scr("s_qkv", [1536, L], BF16)
    scr("s_qr", [512, L], BF16)
    scr("s_kr", [512, L], BF16)
    scr("s_og", [512, L], BF16)
    scr("s_xg", [D, L], BF16)
    scr("s_glu", [2816, L], BF16)
    scr("s_g", [4096, L], BF16)
    K.out["out_fm"] = nc.dram_tensor("out_fm", [D, L], F32, kind="ExternalOutput").ap()


def init_consts(K):
    nc = K.nc
    K.consts["ident_bf"] = K.sb("g_ident_bf", [128, 128], BF16, phase=False)
    K.consts["ident_f"] = K.sb("g_ident_f", [128, 128], F32, phase=False)
    K.consts["mask_geq"] = K.sb("g_mask_geq", [128, 128], F32, phase=False)
    K.consts["mask_leq"] = K.sb("g_mask_leq", [128, 128], F32, phase=False)
    K.consts["pos"] = K.sb("g_pos", [1, L], F32, phase=False)
    K.consts["ones_bf_col"] = K.sb("g_ones_bf_col", [128, 1], BF16, phase=False)
    K.consts["ones_f_row"] = K.sb("g_ones_f_row", [1, 128], F32, phase=False)
    K.consts["negones_f_row"] = K.sb("g_negones_f_row", [1, 128], F32, phase=False)
    K.consts["eights_f_row"] = K.sb("g_eights_f_row", [1, 128], F32, phase=False)
    K.consts["ones_bf_row"] = K.sb("g_ones_bf_row", [1, TT], BF16, phase=False)
    K.consts["r128_bf"] = K.sb("g_r128_bf", [128, 128], BF16, phase=False)
    K.consts["ones128_f"] = K.sb("g_ones128_f", [128, 128], F32, phase=False)
    K.load(K.consts["ident_bf"][:, :], K.inp["c_ident"], cast=True)
    K.load(K.consts["ident_f"][:, :], K.inp["c_ident"])
    K.load(K.consts["mask_geq"][:, :], K.inp["c_mask_geq"])
    K.load(K.consts["mask_leq"][:, :], K.inp["c_mask_leq"])
    K.load(K.consts["pos"][:, :], K.inp["c_pos"])
    K.load(K.consts["r128_bf"][:, :], K.inp["c_r128"], cast=True)
    cvals = [0.0, 1.0, -1.0, -0.5, 0.5, 1e-5, 1e-6, 0.125, -1e30]
    ctile = K.sb("g_constvals", [128, len(cvals)], F32, phase=False)
    for i, v in enumerate(cvals):
        K.dve.memset(ctile[:, i:i + 1], v)
        nc.const_aps.register(F32, v, ctile[:, i:i + 1])
    K.dve.memset(K.consts["ones128_f"][:, :], 1.0)
    K.dve.memset(K.consts["ones_bf_col"][:, :], 1.0)
    K.dve.memset(K.consts["ones_f_row"][:, :], 1.0)
    K.dve.memset(K.consts["negones_f_row"][:, :], -1.0)
    K.dve.memset(K.consts["eights_f_row"][:, :], 8.0)
    K.sem["dve"].inc(K.dve.memset(K.consts["ones_bf_row"][:, :], 1.0))



def emit_sandwich_ln(K, y_dram, x_old_dram, z_dram, xn_bf_dram, xn_f32_dram=None,
                     pass_b_only=False):
    ysb = K.sb("ln_ysb", [128, DC * TT], F32)
    xosb = K.sb("ln_xosb", [128, DC * TT], F32) if (x_old_dram is not None) else None
    zsb = K.sb("ln_zsb", [128, DC * TT], F32)
    zsb2 = K.sb("ln_zsb2", [128, DC * TT], F32)
    xnsb = K.sb("ln_xnsb", [128, DC * TT], BF16) if xn_bf_dram is not None else None
    xnfsb = K.sb("ln_xnfsb", [128, DC * TT], F32) if xn_f32_dram is not None else None
    bfsb = K.sb("ln_bfsb", [128, DC * TT], BF16)
    sqsb = K.sb("ln_sqsb", [128, DC * TT], BF16)
    rows = K.sb("ln_rows", [1, 6 * TT], F32)
    pp_stat = K.pp("ln_ps", [1, TT], n=4)
    pp_bc = K.pp("ln_pb", [128, TT], n=4)
    onesb = K.consts["ones_bf_col"]
    ones_f = K.consts["ones_f_row"]
    negones_f = K.consts["negones_f_row"]

    def ln_pass(src_sb, out_cb, t0):
        if DEV_STAGE < 2:
            return
        K.sem["dve"].wait(K.act)
        K.sem["dve"].inc(K.dve.tensor_copy(bfsb[:, :], src_sb[:, :]))
        K.sem["act"].inc(K.act.activation(sqsb[:, :], src_sb[:, :], AF.Square))
        if DEV_STAGE < 3:
            return
        K.sem["dve"].wait(K.pe)
        K.sem["act"].wait(K.pe)
        ps_sum = pp_stat.take()
        for c in range(DC):
            K.pe.matmul(ps_sum[:, :], onesb[:, :], bfsb[:, c * TT:(c + 1) * TT],
                        start=(c == 0), stop=(c == DC - 1))
        ps_sq = pp_stat.take()
        last = None
        for c in range(DC):
            last = K.pe.matmul(ps_sq[:, :], onesb[:, :], sqsb[:, c * TT:(c + 1) * TT],
                               start=(c == 0), stop=(c == DC - 1))
        K.sem["pe"].inc(last)
        if DEV_STAGE < 4:
            return
        K.sem["pe"].wait(K.dve)
        mean = rows[0:1, 0:TT]
        var = rows[0:1, TT:2 * TT]
        rstd = rows[0:1, 2 * TT:3 * TT]
        negmean = rows[0:1, 3 * TT:4 * TT]
        m2 = rows[0:1, 4 * TT:5 * TT]
        pp_stat.done(K.dve.tensor_scalar(mean, ps_sum[:, :], 1.0 / D, None, ALU.mult))
        inst = K.dve.tensor_scalar(var, ps_sq[:, :], 1.0 / D, None, ALU.mult)
        K.sem["dve"].inc(inst)
        pp_done_real(K, pp_stat, inst)
        K.sem["dve"].wait(K.dve)
        K.sem["dve"].inc(K.dve.tensor_mul(m2, mean, mean))
        K.sem["dve"].wait(K.dve)
        K.sem["dve"].inc(K.dve.tensor_sub(var, var, m2))
        K.sem["dve"].wait(K.dve)
        K.sem["dve"].inc(K.dve.tensor_scalar(negmean, mean, -1.0, None, ALU.mult))
        lnv = rows[0:1, 5 * TT:6 * TT]
        K.sem["dve"].wait(K.act)
        K.sem["act"].inc(K.act.activation(lnv, var, AF.Ln, bias=1e-5))
        K.sem["act"].wait(K.act)
        K.sem["act"].inc(K.act.activation(rstd, lnv, AF.Exp, scale=-0.5))
        if DEV_STAGE < 5:
            return
        K.sem["act"].wait(K.pe)
        negmB = pp_bc.take()
        K.pe.matmul(negmB[:, :], ones_f[:, :], negmean, start=True, stop=True)
        rstdB = pp_bc.take()
        K.sem["pe"].inc(K.pe.matmul(rstdB[:, :], ones_f[:, :], rstd, start=True, stop=True))
        K.sem["pe"].wait(K.dve)
        if DEV_STAGE < 6:
            pp_bc.done(K.dve.nop(), 2)
            return
        out_cb(negmB, rstdB)

    for t in range(NT if DEV_TILES is None else DEV_TILES):
        t0 = t * TT
        K.sem["dve"].wait(K.sp)
        K.sem["pe"].wait(K.sp)
        K.load(ysb.ap().rearrange("p (c t) -> p c t", c=DC), K.tile_ap(y_dram, t0, TT))
        if xosb is not None:
            K.load(xosb.ap().rearrange("p (c t) -> p c t", c=DC), K.tile_ap(x_old_dram, t0, TT))
        K.wait_loads(K.dve)
        K.wait_loads(K.act)
        K.wait_loads(K.pe)
        K.wait_stores(K.dve)
        K.sem["pe"].wait(K.dve)
        K.sem["pe"].wait(K.act)

        if not pass_b_only:
            def cb_a(negmB, rstdB):
                last = None
                for c in range(DC):
                    cs = slice(c * TT, (c + 1) * TT)
                    K.dve.tensor_add(zsb[:, cs], ysb[:, cs], negmB[:, :])
                    K.dve.tensor_mul(zsb[:, cs], zsb[:, cs], rstdB[:, :])
                    last = K.dve.tensor_add(zsb[:, cs], zsb[:, cs], xosb[:, cs])
                K.sem["dve"].inc(last)
                K.bump("dve", pp_bc.sem, 2)
                pp_bc.dones += 2
            ln_pass(ysb, cb_a, t0)
            if z_dram is not None and DEV_STAGE >= 7:
                K.sem["dve"].wait(K.sp)
                K.store(K.tile_ap(z_dram, t0, TT),
                        zsb.ap().rearrange("p (c t) -> p c t", c=DC), eng=K.sp)
            src_b = zsb
        else:
            src_b = ysb

        if xn_bf_dram is not None or xn_f32_dram is not None:
            def cb_b(negmB, rstdB):
                last = None
                for c in range(DC):
                    cs = slice(c * TT, (c + 1) * TT)
                    if xnfsb is not None:
                        K.dve.tensor_add(xnfsb[:, cs], src_b[:, cs], negmB[:, :])
                        last = K.dve.tensor_mul(xnfsb[:, cs], xnfsb[:, cs], rstdB[:, :])
                        if xnsb is not None:
                            last = K.dve.tensor_copy(xnsb[:, cs], xnfsb[:, cs])
                    else:
                        K.dve.tensor_add(zsb2[:, cs], src_b[:, cs], negmB[:, :])
                        last = K.dve.tensor_mul(xnsb[:, cs], zsb2[:, cs], rstdB[:, :])
                K.sem["dve"].inc(last)
                K.bump("dve", pp_bc.sem, 2)
                pp_bc.dones += 2
            ln_pass(src_b, cb_b, t0)
            K.sem["dve"].wait(K.pool)
            K.sem["dve"].wait(K.sp)
            if DEV_STAGE < 7:
                continue
            if xn_bf_dram is not None:
                K.store(K.tile_ap(xn_bf_dram, t0, TT),
                        xnsb.ap().rearrange("p (c t) -> p c t", c=DC), eng=K.sp)
            if xn_f32_dram is not None:
                K.store(K.tile_ap(xn_f32_dram, t0, TT),
                        xnfsb.ap().rearrange("p (c t) -> p c t", c=DC), eng=K.sp)




def emit_proj(K, name, w_dram, Kd, N, x_dram, out_dram=None, out_dtype=BF16,
              bias_row_dram=None, x_cast=False, evac=None, resid_dram=None,
              out_scale=None, x_loader=None, xsb=None, pp=None, store_cb=None):
    KC = Kd // 128
    NC = (N + 127) // 128
    wsb = K.sb(f"{name}_w", [128, KC * N], BF16)
    for kc in range(KC):
        K.load(wsb[:, kc * N:(kc + 1) * N], w_dram[kc * 128:(kc + 1) * 128, :], cast=True)
    brow = None
    if bias_row_dram is not None:
        brow = K.sb(f"{name}_bias", [1, N], BF16)
        K.load(brow[:, :], bias_row_dram, cast=True)
    xsb2 = None
    if xsb is None:
        xsb = K.sb(f"{name}_x", [128, KC * TT], BF16)
        xsb2 = K.sb(f"{name}_x2", [128, KC * TT], BF16)
    osb = K.sb(f"{name}_o", [128, NC * TT], out_dtype)
    rsb = K.sb(f"{name}_r", [128, NC * TT], F32) if resid_dram is not None else None
    own_pp = pp is None
    if own_pp:
        pp = K.pp(f"{name}_pp", [128, TT], n=4)
    onesr = K.consts["ones_bf_row"]
    pe_marks = []
    for t in range(NT):
        t0 = t * TT
        if x_loader is not None:
            x_loader(t)
            xb = xsb
        else:
            xb = xsb if t % 2 == 0 else xsb2
            if t >= 2:
                K.sp.wait_ge(K.sem["pe"].h, pe_marks[t - 2])
                K.pool.wait_ge(K.sem["pe"].h, pe_marks[t - 2])
            K.load(xb.ap().rearrange("p (c t) -> p c t", c=KC),
                   K.tile_ap(x_dram, t0, TT, chunks=KC), cast=x_cast)
        if rsb is not None:
            K.sem["dve"].wait(K.sp)
            K.load(rsb.ap().rearrange("p (c t) -> p c t", c=NC),
                   K.tile_ap(resid_dram, t0, TT, chunks=NC))
        K.wait_loads(K.pe)
        K.wait_stores(K.dve)
        K.wait_stores(K.act)
        for ncx in range(NC):
            nw = min(128, N - ncx * 128)
            ps = pp.take()
            first = True
            if brow is not None:
                K.pe.matmul(ps[0:nw, :], brow[0:1, ncx * 128:ncx * 128 + nw],
                            onesr[:, :], start=True, stop=False)
                first = False
            last = None
            for kc in range(KC):
                last = K.pe.matmul(ps[0:nw, :],
                                   wsb[:, kc * N + ncx * 128: kc * N + ncx * 128 + nw],
                                   xb[:, kc * TT:(kc + 1) * TT],
                                   start=first, stop=(kc == KC - 1))
                first = False
            K.sem["pe"].inc(last)
            K.sem["pe"].wait(K.dve)
            cs = slice(ncx * TT, ncx * TT + TT)
            if evac is not None:
                evac(ncx, nw, t, ps, osb, pp)
            elif rsb is not None:
                inst = K.dve.tensor_add(osb[0:nw, cs], ps[0:nw, :], rsb[0:nw, cs])
                K.sem["dve"].inc(inst)
                pp_done_real(K, pp, inst)
            elif out_scale is not None:
                inst = K.dve.tensor_scalar(osb[0:nw, cs], ps[0:nw, :], out_scale,
                                           None, ALU.mult)
                K.sem["dve"].inc(inst)
                pp_done_real(K, pp, inst)
            else:
                inst = K.dve.tensor_copy(osb[0:nw, cs], ps[0:nw, :])
                K.sem["dve"].inc(inst)
                pp_done_real(K, pp, inst)
        pe_marks.append(K.sem["pe"].n)
        if store_cb is not None:
            store_cb(t, osb)
        elif out_dram is not None:
            K.sem["dve"].wait(K.sp)
            K.sem["act"].wait(K.sp)
            K.store(K.tile_ap(out_dram, t0, TT, chunks=NC),
                    osb.ap().rearrange("p (c t) -> p c t", c=NC), eng=K.sp)



def emit_l0attn(K):
    old_es = K.phase_es
    beta_tm = K.sb("l0a_beta_tm", [128, 32 * 16], F32)
    fw_tm = K.sb("l0a_fw_tm", [128, 32 * 16], F32)
    sw_tm = K.sb("l0a_sw_tm", [128, 32 * 16], F32)
    invB = K.sb("l0a_invB", [128, 32], F32)
    btf1B = K.sb("l0a_btf1B", [128, 32], F32)
    w2B = K.sb("l0a_w2B", [128, 32], F32)
    btf2B = K.sb("l0a_btf2B", [128, 1], F32)
    wtf1 = K.sb("l0a_wtf1", [128, 32], BF16)
    K.maps_b = (beta_tm, fw_tm, sw_tm, invB, btf1B, w2B, btf2B)
    K.wtf1 = wtf1
    K.load(wtf1[0:64, :], K.inp["w_l0a_Wtf1"], cast=True)
    K.load(wtf1[64:128, :], K.inp["w_l0a_Wtf1"], cast=True)
    with ExitStack() as es_ab:
        K.phase_es = es_ab
        beta_fm = K.sb("l0a_beta", [16, L], F32)
        fd_fm = K.sb("l0a_fd", [16, L], F32)
        sd_fm = K.sb("l0a_sd", [16, L], F32)
        with ExitStack() as es_a1:
            K.phase_es = es_a1
            emit_l0attn_proj(K, beta_fm, fd_fm, sd_fm)
            K.phase_es = es_ab
        K.barrier()
        with ExitStack() as es_b1:
            K.phase_es = es_b1
            emit_l0attn_decay(K, beta_fm, fd_fm, sd_fm)
            K.phase_es = es_ab
        K.barrier()
    K.phase_es = old_es
    with ExitStack() as es_c:
        K.phase_es = es_c
        emit_l0attn_pairs(K)
        K.phase_es = old_es
    K.barrier()
    with ExitStack() as es_d:
        K.phase_es = es_d
        emit_proj(K, "l0a_wo", K.inp["w_l0a_Wo"], D, D, K.scr["s_o"],
                  out_dram=K.scr["s_a"], out_dtype=F32)
        K.phase_es = old_es


def emit_l0attn_proj(K, beta_fm, fd_fm, sd_fm):
    xnb = K.scr["s_xnb"]
    brow2 = K.sb("l0a_brow2", [1, 32], F32)
    bcol2 = K.sb("l0a_bcol2", [16, 2], F32)
    K.load(brow2[0:1, 0:16], K.inp["w_l0a_bfd"])
    K.load(brow2[0:1, 16:32], K.inp["w_l0a_bsd"])
    ppf = K.pp("l0a_ppA", [128, TT], n=4)
    K.wait_loads(K.pe)
    for i in range(2):
        ps = ppf.take()
        inst = K.pe.transpose(ps[0:16, 0:1], brow2[0:1, 16 * i:16 * i + 16],
                              K.consts["ones_f_row"][0:1, 0:1])
        K.sem["pe"].inc(inst)
        K.sem["pe"].wait(K.dve)
        inst = K.dve.tensor_copy(bcol2[:, i:i + 1], ps[0:16, 0:1])
        K.sem["dve"].inc(inst)
        pp_done_real(K, ppf, inst)
    wq = K.sb("l0a_wq", [128, DC * D], BF16)
    wk = K.sb("l0a_wk", [128, DC * D], BF16)
    wv = K.sb("l0a_wv", [128, DC * D], BF16)
    wmap = K.sb("l0a_wmap", [128, DC * 48], BF16)
    for kc in range(DC):
        K.load(wq[:, kc * D:(kc + 1) * D], K.inp["w_l0a_Wq"][kc * 128:(kc + 1) * 128, :], cast=True)
        K.load(wk[:, kc * D:(kc + 1) * D], K.inp["w_l0a_Wk"][kc * 128:(kc + 1) * 128, :], cast=True)
        K.load(wv[:, kc * D:(kc + 1) * D], K.inp["w_l0a_Wv"][kc * 128:(kc + 1) * 128, :], cast=True)
        K.load(wmap[:, kc * 48 + 0:kc * 48 + 16], K.inp["w_l0a_Wb"][kc * 128:(kc + 1) * 128, :], cast=True)
        K.load(wmap[:, kc * 48 + 16:kc * 48 + 32], K.inp["w_l0a_Wfd"][kc * 128:(kc + 1) * 128, :], cast=True)
        K.load(wmap[:, kc * 48 + 32:kc * 48 + 48], K.inp["w_l0a_Wsd"][kc * 128:(kc + 1) * 128, :], cast=True)
    xsb = K.sb("l0a_x", [128, DC * TT], BF16)
    osb = K.sb("l0a_osb", [128, 3 * DC * TT], BF16)
    for t in range(NT):
        t0 = t * TT
        K.sem["pe"].wait(K.sp)
        K.wait_stores(K.sp)
        K.load(xsb.ap().rearrange("p (c t) -> p c t", c=DC), K.tile_ap(xnb, t0, TT))
        K.wait_loads(K.pe)
        for pi, w in enumerate((wq, wk, wv)):
            for ncx in range(DC):
                ps = ppf.take()
                last = None
                for kc in range(DC):
                    last = K.pe.matmul(ps[:, :],
                                       w[:, kc * D + ncx * 128:kc * D + ncx * 128 + 128],
                                       xsb[:, kc * TT:(kc + 1) * TT],
                                       start=(kc == 0), stop=(kc == DC - 1))
                K.sem["pe"].inc(last)
                K.sem["pe"].wait(K.dve)
                inst = K.dve.tensor_copy(osb[:, (pi * DC + ncx) * TT:(pi * DC + ncx + 1) * TT],
                                         ps[:, :])
                K.sem["dve"].inc(inst)
                pp_done_real(K, ppf, inst)
        for mi, (mfm, bias) in enumerate(((beta_fm, None), (fd_fm, 0), (sd_fm, 1))):
            ps = ppf.take()
            last = None
            for kc in range(DC):
                last = K.pe.matmul(ps[0:16, :],
                                   wmap[:, kc * 48 + mi * 16:kc * 48 + mi * 16 + 16],
                                   xsb[:, kc * TT:(kc + 1) * TT],
                                   start=(kc == 0), stop=(kc == DC - 1))
            K.sem["pe"].inc(last)
            K.sem["pe"].wait(K.act)
            if bias is None:
                inst = K.act.activation(mfm[0:16, t0:t0 + TT], ps[0:16, :], AF.Sigmoid)
            else:
                inst = K.act.activation(mfm[0:16, t0:t0 + TT], ps[0:16, :], AF.Sigmoid,
                                        bias=bcol2[:, bias:bias + 1])
            K.sem["act"].inc(inst)
            pp_done_real(K, ppf, inst, eng="act")
        K.sem["dve"].wait(K.sp)
        for pi, dst in enumerate(("s_q", "s_k", "s_v")):
            K.store(K.tile_ap(K.scr[dst], t0, TT),
                    osb[:, pi * DC * TT:(pi + 1) * DC * TT].rearrange("p (c t) -> p c t", c=DC),
                    eng=K.sp)


def emit_l0attn_decay(K, beta_fm, fd_fm, sd_fm):
    beta_tm, fw_tm, sw_tm, invB, btf1B, w2B, btf2B = K.maps_b
    logf = K.sb("l0b_logf", [16, L], F32)
    cs_f = K.sb("l0b_cs", [16, L], F32)
    fw_fm = K.sb("l0b_fwfm", [16, L], F32)
    sw_fm = K.sb("l0b_swfm", [16, L], F32)
    zeros16 = K.sb("l0b_z16", [16, L], F32)
    sums = K.sb("l0b_sums", [16, 4], F32)
    rowtmp = K.sb("l0b_rowtmp", [1, 64], F32)
    brow3 = K.sb("l0b_brow3", [1, 33], F32)
    ppf2 = K.pp("l0b_pp", [128, TT], n=4)
    K.sem["dve"].inc(K.dve.memset(zeros16[:, :], 0.0))
    for si, (src_map, dst_fm, sumcol) in enumerate(((fd_fm, fw_fm, 0), (sd_fm, sw_fm, 2))):
        K.sem["act"].inc(K.act.activation(logf[:, :], src_map[0:16, :], AF.Ln, bias=1e-6))
        K.sem["act"].wait(K.dve)
        K.sem["dve"].inc(K.dve.tensor_tensor_scan(cs_f[:, :], logf[:, :], zeros16[:, :],
                                                  0.0, ALU.add, ALU.add))
        K.sem["dve"].wait(K.act)
        K.sem["act"].inc(K.act.activation(dst_fm[:, :], cs_f[:, :], AF.Exp, scale=-1.0,
                                          bias=cs_f[0:16, L - 1:L],
                                          accum_out=sums[:, sumcol:sumcol + 1]))
        K.sem["act"].wait(K.dve)
        K.sem["dve"].inc(K.dve.tensor_scalar(sums[:, sumcol + 1:sumcol + 2],
                                             sums[:, sumcol:sumcol + 1], 1e-6, None, ALU.add))
        K.sem["dve"].wait(K.dve)
        K.sem["dve"].inc(K.dve.reciprocal(sums[:, sumcol + 1:sumcol + 2],
                                          sums[:, sumcol + 1:sumcol + 2]))
    K.sem["dve"].wait(K.pe)
    K.sem["act"].wait(K.pe)
    for mfm, mtm in ((beta_fm, beta_tm), (fw_fm, fw_tm), (sw_fm, sw_tm)):
        for c in range(32):
            ps = ppf2.take()
            inst = K.pe.transpose(ps[:, 0:16], mfm[0:16, c * 128:(c + 1) * 128],
                                  K.consts["ident_f"][0:16, 0:16])
            K.sem["pe"].inc(inst)
            K.sem["pe"].wait(K.dve)
            inst = K.dve.tensor_copy(mtm[:, c * 16:(c + 1) * 16], ps[:, 0:16])
            K.sem["dve"].inc(inst)
            pp_done_real(K, ppf2, inst)
    for si in range(2):
        ps = ppf2.take()
        inst = K.pe.transpose(ps[0:1, 0:16], sums[0:16, 2 * si + 1:2 * si + 2],
                              K.consts["ident_f"][0:16, 0:16])
        K.sem["pe"].inc(inst)
        K.sem["pe"].wait(K.dve)
        inst = K.dve.tensor_copy(rowtmp[0:1, 16 * si:16 * si + 16], ps[0:1, 0:16])
        K.sem["dve"].inc(inst)
        pp_done_real(K, ppf2, inst)
        K.sem["dve"].wait(K.pe)
        ps = ppf2.take()
        inst = K.pe.matmul(ps[:, 0:16], K.consts["ones_f_row"][:, :],
                           rowtmp[0:1, 16 * si:16 * si + 16], start=True, stop=True)
        K.sem["pe"].inc(inst)
        K.sem["pe"].wait(K.dve)
        inst = K.dve.tensor_copy(invB[:, 16 * si:16 * si + 16], ps[:, 0:16])
        K.sem["dve"].inc(inst)
        pp_done_real(K, ppf2, inst)
    K.load(rowtmp[0:1, 32:64], K.inp["w_l0a_btf1"])
    K.load(brow3[0:1, 0:32], K.inp["w_l0a_wtf2"])
    K.load(brow3[0:1, 32:33], K.inp["w_l0a_btf2"])
    K.wait_loads(K.pe)
    for dst, rowap, w in ((btf1B, rowtmp[0:1, 32:64], 32), (w2B, brow3[0:1, 0:32], 32),
                          (btf2B, brow3[0:1, 32:33], 1)):
        ps = ppf2.take()
        inst = K.pe.matmul(ps[:, 0:w], K.consts["ones_f_row"][:, :], rowap,
                           start=True, stop=True)
        K.sem["pe"].inc(inst)
        K.sem["pe"].wait(K.dve)
        inst = K.dve.tensor_copy(dst[:, 0:w], ps[:, 0:w])
        K.sem["dve"].inc(inst)
        pp_done_real(K, ppf2, inst)


def pp_done_real(K, pp, inst, eng="dve", k=1):
    K.bump(eng, pp.sem, k)
    pp.dones += k


def emit_l0attn_pairs(K):
    beta_tm, fw_tm, sw_tm, invB, btf1B, w2B, btf2B = K.maps_b
    q_p = K.sb("l0c_q", [128, L], BF16)
    k_p = K.sb("l0c_k", [128, L], BF16)
    v_p = K.sb("l0c_v", [128, L], BF16)
    k_tm = K.sb("l0c_ktm", [128, L], BF16)
    v_tm = K.sb("l0c_vtm", [128, L], BF16)
    sqq = K.sb("l0c_sqq", [128, L], BF16)
    sqk = K.sb("l0c_sqk", [128, L], BF16)
    rowsb = K.sb("l0c_row", [1, TT], F32)
    rq_raw = K.sb("l0c_rq", [128, 64], F32)
    rk_raw = K.sb("l0c_rk", [128, 64], F32)
    ecol = K.sb("l0c_ecol", [128, 64 * 3], F32)
    kf = K.sb("l0c_kf", [128, L], BF16)
    ks2 = K.sb("l0c_ks2", [128, L], BF16)
    vs = K.sb("l0c_vs", [128, L], BF16)
    pre_fm = K.sb("l0c_prefm", [32, 2 * L], F32)
    pre_tm = K.sb("l0c_pretm", [128, 2048], F32)
    silu_sb = K.sb("l0c_silu", [128, 2048], F32)
    tfc = K.sb("l0c_tfc", [128, 64], F32)
    c1t = K.sb("l0c_c1", [128, 64], F32)
    c2t = K.sb("l0c_c2", [128, 64], F32)
    msb = K.sb("l0c_m", [128, 128], BF16)
    o_tm = K.sb("l0c_otm", [128, L], BF16)
    o_tm2 = K.sb("l0c_otm2", [128, L], BF16)
    o_fm = K.sb("l0c_ofm", [128, L], BF16)
    ppf = K.pp("l0c_ppf", [128, TT], n=4)
    ppb = K.pp("l0c_ppb", [128, 128], n=2, dtype=BF16)
    ppm = K.pp("l0c_ppm", [128, 64], n=2)
    onesb = K.consts["ones_bf_col"]
    identb = K.consts["ident_bf"]
    for p in range(8):
        for eng in (K.sp,):
            K.sem["pe"].wait(eng)
            K.sem["dve"].wait(eng)
            K.sem["act"].wait(eng)
            K.wait_stores(eng)
        K.load(q_p[:, :], K.scr["s_q"][p * 128:(p + 1) * 128, :])
        K.load(k_p[:, :], K.scr["s_k"][p * 128:(p + 1) * 128, :])
        K.load(v_p[:, :], K.scr["s_v"][p * 128:(p + 1) * 128, :])
        K.wait_loads(K.pe)
        K.wait_loads(K.act)
        for src, dst in ((k_p, k_tm), (v_p, v_tm)):
            for c in range(32):
                ps = ppb.take()
                inst = K.pe.transpose(ps[:, :], src[:, c * 128:(c + 1) * 128], identb[:, :])
                K.sem["pe"].inc(inst)
                K.sem["pe"].wait(K.dve)
                inst = K.dve.tensor_copy(dst[:, c * 128:(c + 1) * 128], ps[:, :])
                K.sem["dve"].inc(inst)
                pp_done_real(K, ppb, inst)
        K.sem["act"].inc(K.act.activation(sqq[:, :], q_p[:, :], AF.Square))
        K.sem["act"].inc(K.act.activation(sqk[:, :], k_p[:, :], AF.Square))
        K.sem["act"].wait(K.pe)
        for sq, raw in ((sqq, rq_raw), (sqk, rk_raw)):
            for hh in range(2):
                for t in range(NT):
                    ps = ppf.take()
                    inst = K.pe.matmul(ps[0:1, :], onesb[hh * 64:(hh + 1) * 64, 0:1],
                                       sq[hh * 64:(hh + 1) * 64, t * TT:(t + 1) * TT],
                                       start=True, stop=True)
                    K.sem["pe"].inc(inst)
                    K.sem["pe"].wait(K.dve)
                    inst = K.dve.tensor_copy(rowsb[0:1, :], ps[0:1, :])
                    K.sem["dve"].inc(inst)
                    pp_done_real(K, ppf, inst)
                    K.sem["dve"].wait(K.pe)
                    for cc in range(4):
                        c = t * 4 + cc
                        ps2 = ppf.take()
                        inst = K.pe.transpose(ps2[:, 0:1], rowsb[0:1, cc * 128:(cc + 1) * 128],
                                              K.consts["ones_f_row"][0:1, 0:1])
                        K.sem["pe"].inc(inst)
                        K.sem["pe"].wait(K.dve)
                        inst = K.dve.tensor_copy(raw[:, c * 2 + hh:c * 2 + hh + 1], ps2[:, 0:1])
                        K.sem["dve"].inc(inst)
                        pp_done_real(K, ppf, inst)
        K.sem["dve"].wait(K.dve)
        for raw in (rq_raw, rk_raw):
            K.sem["dve"].inc(K.dve.tensor_scalar(raw[:, :], raw[:, :], 1e-24, None, ALU.max))
            K.sem["dve"].wait(K.act)
            K.sem["act"].inc(K.act.activation(raw[:, :], raw[:, :], AF.Ln))
            K.sem["act"].wait(K.act)
            K.sem["act"].inc(K.act.activation(raw[:, :], raw[:, :], AF.Exp, scale=-0.5))
        K.sem["act"].wait(K.dve)
        K.sem["act"].wait(K.pe)
        for hh in range(2):
            for t in range(NT):
                ps = ppf.take()
                inst = K.pe.matmul(ps[0:32, :], K.wtf1[hh * 64:(hh + 1) * 64, :],
                                   k_p[hh * 64:(hh + 1) * 64, t * TT:(t + 1) * TT],
                                   start=True, stop=True)
                K.sem["pe"].inc(inst)
                K.sem["pe"].wait(K.dve)
                inst = K.dve.tensor_copy(pre_fm[0:32, hh * L + t * TT:hh * L + (t + 1) * TT],
                                         ps[0:32, :])
                K.sem["dve"].inc(inst)
                pp_done_real(K, ppf, inst)
        K.sem["dve"].wait(K.pe)
        for c in range(32):
            for hh in range(2):
                ps = ppf.take()
                inst = K.pe.transpose(ps[:, 0:32],
                                      pre_fm[0:32, hh * L + c * 128:hh * L + (c + 1) * 128],
                                      K.consts["ident_f"][0:32, 0:32])
                K.sem["pe"].inc(inst)
                K.sem["pe"].wait(K.dve)
                inst = K.dve.tensor_copy(pre_tm[:, (c * 2 + hh) * 32:(c * 2 + hh + 1) * 32],
                                         ps[:, 0:32])
                K.sem["dve"].inc(inst)
                pp_done_real(K, ppf, inst)
        last = None
        for c in range(32):
            for hh in range(2):
                h = 2 * p + hh
                e = (c * 2 + hh) * 3
                last = K.dve.tensor_mul(ecol[:, e:e + 1], rk_raw[:, c * 2 + hh:c * 2 + hh + 1],
                                        beta_tm[:, c * 16 + h:c * 16 + h + 1])
        K.sem["dve"].inc(last)
        K.sem["dve"].wait(K.dve)
        last = None
        for c in range(32):
            for hh in range(2):
                h = 2 * p + hh
                e = (c * 2 + hh) * 3
                K.dve.tensor_mul(ecol[:, e + 1:e + 2], ecol[:, e:e + 1],
                                 fw_tm[:, c * 16 + h:c * 16 + h + 1])
                last = K.dve.tensor_mul(ecol[:, e + 2:e + 3], ecol[:, e:e + 1],
                                        sw_tm[:, c * 16 + h:c * 16 + h + 1])
        K.sem["dve"].inc(last)
        K.sem["dve"].wait(K.dve)
        last = None
        for c in range(32):
            for hh in range(2):
                h = 2 * p + hh
                e = (c * 2 + hh) * 3
                sl = slice(c * 128 + hh * 64, c * 128 + hh * 64 + 64)
                K.dve.tensor_scalar(kf[:, sl], k_tm[:, sl], ecol[:, e + 1:e + 2], None, ALU.mult)
                K.dve.tensor_scalar(ks2[:, sl], k_tm[:, sl], ecol[:, e + 2:e + 3], None, ALU.mult)
                K.dve.tensor_scalar(vs[:, sl], v_tm[:, sl],
                                    beta_tm[:, c * 16 + h:c * 16 + h + 1], None, ALU.mult)
                ts_ = slice((c * 2 + hh) * 32, (c * 2 + hh + 1) * 32)
                last = K.dve.tensor_scalar(pre_tm[:, ts_], pre_tm[:, ts_], ecol[:, e:e + 1],
                                           None, ALU.mult)
        K.sem["dve"].inc(last)
        K.sem["dve"].wait(K.dve)
        last = None
        for i in range(64):
            last = K.dve.tensor_add(pre_tm[:, i * 32:(i + 1) * 32],
                                    pre_tm[:, i * 32:(i + 1) * 32], btf1B[:, 0:32])
        K.sem["dve"].inc(last)
        K.sem["dve"].wait(K.act)
        K.sem["act"].inc(K.act.activation(silu_sb[:, :], pre_tm[:, :], AF.Silu))
        K.sem["act"].wait(K.dve)
        last = None
        for i in range(64):
            last = K.dve.tensor_mul(silu_sb[:, i * 32:(i + 1) * 32],
                                    silu_sb[:, i * 32:(i + 1) * 32], w2B[:, 0:32])
        K.sem["dve"].inc(last)
        K.sem["dve"].wait(K.dve)
        last = None
        for i in range(64):
            last = K.dve.reduce_sum(tfc[:, i:i + 1], silu_sb[:, i * 32:(i + 1) * 32], axis=AX.X)
        K.sem["dve"].inc(last)
        K.sem["dve"].wait(K.act)
        K.sem["act"].inc(K.act.activation(tfc[:, :], tfc[:, :], AF.Sigmoid, bias=btf2B[:, 0:1]))
        K.sem["act"].wait(K.dve)
        K.sem["dve"].inc(K.dve.tensor_scalar(tfc[:, :], tfc[:, :], 0.99, None, ALU.min))
        K.sem["dve"].wait(K.dve)
        K.sem["dve"].inc(K.dve.tensor_scalar(tfc[:, :], tfc[:, :], 0.01, None, ALU.max))
        K.sem["dve"].wait(K.dve)
        K.sem["dve"].inc(K.dve.tensor_scalar(c1t[:, :], tfc[:, :], 0.3, 0.5, ALU.mult, ALU.add))
        K.sem["dve"].wait(K.dve)
        K.sem["dve"].inc(K.dve.tensor_scalar(c2t[:, :], c1t[:, :], -1.0, 1.0, ALU.mult, ALU.add))
        K.sem["dve"].wait(K.dve)
        K.dve.tensor_mul(c1t[:, :], c1t[:, :], rq_raw[:, :])
        K.sem["dve"].inc(K.dve.tensor_mul(c2t[:, :], c2t[:, :], rq_raw[:, :]))
        K.sem["dve"].wait(K.dve)
        last = None
        for hh in range(2):
            h = 2 * p + hh
            K.dve.tensor_scalar(c1t[:, hh::2], c1t[:, hh::2], invB[:, h:h + 1], None, ALU.mult)
            last = K.dve.tensor_scalar(c2t[:, hh::2], c2t[:, hh::2],
                                       invB[:, 16 + h:16 + h + 1], None, ALU.mult)
        K.sem["dve"].inc(last)
        K.sem["dve"].wait(K.pe)
        K.sem["dve"].wait(K.dve)
        for hh in range(2):
            for mi, src in enumerate((kf, ks2)):
                ps = ppm.take()
                last = None
                for c in range(32):
                    sl = slice(c * 128 + hh * 64, c * 128 + hh * 64 + 64)
                    last = K.pe.matmul(ps[0:64, :], src[:, sl], vs[:, sl],
                                       start=(c == 0), stop=(c == 31))
                K.sem["pe"].inc(last)
                K.sem["pe"].wait(K.dve)
                inst = K.dve.tensor_copy(msb[hh * 64:(hh + 1) * 64, mi * 64:(mi + 1) * 64],
                                         ps[0:64, :])
                K.sem["dve"].inc(inst)
                pp_done_real(K, ppm, inst)
        if DEV_DBGPAIR:
            K.sem["dve"].wait(K.pool)
            K.store(K.scr["s_y"][0:128, p * 128:(p + 1) * 128], msb[:, :])
            K.store(K.scr["s_y"][128:256, p * 64:(p + 1) * 64], c1t[:, :])
            K.store(K.scr["s_y"][256:384, p * 64:(p + 1) * 64], c2t[:, :])
            K.store(K.scr["s_y"][384:512, p * 192:(p + 1) * 192], ecol[:, :])
            K.store(K.scr["s_y"][512:640, p * 64:(p + 1) * 64], rq_raw[:, :])
        K.sem["dve"].wait(K.pe)
        for c in range(32):
            for hh in range(2):
                psf = ppm.take()
                K.pe.matmul(psf[:, :], q_p[hh * 64:(hh + 1) * 64, c * 128:(c + 1) * 128],
                            msb[hh * 64:(hh + 1) * 64, 0:64], start=True, stop=True)
                pso = ppm.take()
                inst = K.pe.matmul(pso[:, :], q_p[hh * 64:(hh + 1) * 64, c * 128:(c + 1) * 128],
                                   msb[hh * 64:(hh + 1) * 64, 64:128],
                                   start=True, stop=True)
                K.sem["pe"].inc(inst)
                K.sem["pe"].wait(K.dve)
                sl = slice(c * 128 + hh * 64, c * 128 + hh * 64 + 64)
                K.dve.tensor_scalar(o_tm[:, sl], psf[:, :], c1t[:, c * 2 + hh:c * 2 + hh + 1],
                                    None, ALU.mult)
                inst = K.dve.tensor_scalar(o_tm2[:, sl], pso[:, :],
                                           c2t[:, c * 2 + hh:c * 2 + hh + 1], None, ALU.mult)
                K.sem["dve"].inc(inst)
                pp_done_real(K, ppm, inst, k=2)
        K.sem["dve"].wait(K.dve)
        K.sem["dve"].inc(K.dve.tensor_add(o_tm[:, :], o_tm[:, :], o_tm2[:, :]))
        K.sem["dve"].wait(K.pe)
        for c in range(32):
            ps = ppb.take()
            inst = K.pe.transpose(ps[:, :], o_tm[:, c * 128:(c + 1) * 128], identb[:, :])
            K.sem["pe"].inc(inst)
            K.sem["pe"].wait(K.dve)
            inst = K.dve.tensor_copy(o_fm[:, c * 128:(c + 1) * 128], ps[:, :])
            K.sem["dve"].inc(inst)
            pp_done_real(K, ppb, inst)
        K.sem["dve"].wait(K.pool)
        K.store(K.scr["s_o"][p * 128:(p + 1) * 128, :], o_fm[:, :])




def _shift_loader(K, src_dram, xsb, KC):
    half = KC // 2
    def load(t):
        t0 = t * TT
        K.sem["pe"].wait(K.sp)
        K.sem["pe"].wait(K.pool)
        K.wait_stores(K.sp)
        K.load(xsb.ap().rearrange("p (c t) -> p c t", c=KC)[:, 0:half, :],
               K.tile_ap(src_dram, t0, TT, chunks=KC)[:, 0:half, :], cast=True)
        if t == 0:
            K.sem["pe"].wait(K.dve)
            for c in range(half, KC):
                K.sem["dve"].inc(K.dve.memset(xsb[:, c * TT:c * TT + 1], 0.0))
                K.load(xsb[:, c * TT + 1:(c + 1) * TT],
                       src_dram[c * 128:(c + 1) * 128, 0:TT - 1], cast=True)
            K.sem["dve"].wait(K.pe)
        else:
            K.load(xsb.ap().rearrange("p (c t) -> p c t", c=KC)[:, half:KC, :],
                   K.tile_ap(src_dram, t0 - 1, TT, chunks=KC)[:, half:KC, :], cast=True)
    return load


def emit_l0ffn_layer(K, l):
    xin = K.scr["s_xn"] if l == 0 else K.scr["s_y2"]
    yout2 = K.scr["s_y2"] if l == 0 else K.scr["s_a"]
    old_es = K.phase_es
    with ExitStack() as es:
        K.phase_es = es
        xsb = K.sb("ffA_x", [128, DC * TT], BF16)
        emit_proj(K, "ffA", K.inp[f"w_ff{l}_wqkv"], D, 1536, None,
                  out_dram=K.scr["s_qkv"], out_dtype=BF16,
                  x_loader=_shift_loader(K, xin, xsb, DC), xsb=xsb)
        K.phase_es = old_es
    K.barrier()
    with ExitStack() as es:
        K.phase_es = es
        emit_l0ffn_rotary(K, l)
        K.phase_es = old_es
    K.barrier()
    with ExitStack() as es:
        K.phase_es = es
        emit_l0ffn_attn(K, l)
        K.phase_es = old_es
    K.barrier()
    with ExitStack() as es:
        K.phase_es = es
        emit_proj(K, "ffD", K.inp[f"w_ff{l}_wo"], 512, D, K.scr["s_og"],
                  out_dram=K.scr["s_y"], out_dtype=F32, resid_dram=xin)
        K.phase_es = old_es
    K.barrier()
    with ExitStack() as es:
        K.phase_es = es
        emit_l0ffn_ln(K, l)
        K.phase_es = old_es
    K.barrier()
    with ExitStack() as es:
        K.phase_es = es
        emit_l0ffn_glu(K, l)
        K.phase_es = old_es
    K.barrier()
    with ExitStack() as es:
        K.phase_es = es
        emit_proj(K, "ffG", K.inp[f"w_ff{l}_w2"], 2816, D, K.scr["s_glu"],
                  out_dram=yout2, out_dtype=F32, resid_dram=K.scr["s_y"],
                  bias_row_dram=K.inp[f"w_ff{l}_b2"])
        K.phase_es = old_es


def emit_l0ffn_rotary(K, l):
    maps = {}
    cols = K.sb("ffB_cols", [128, 4], F32)
    rowt = K.sb("ffB_rowt", [1, 64], F32)
    ppf = K.pp("ffB_pp", [128, TT], n=4)
    for i, nm in enumerate(("qs", "qsp", "ks", "ksp")):
        K.sem["pe"].wait(K.sp)
        K.load(rowt[0:1, :], K.inp[f"w_ff{l}_{nm}"])
        K.wait_loads(K.pe)
        K.sem["dve"].wait(K.pe)
        ps = ppf.take()
        inst = K.pe.transpose(ps[0:64, 0:1], rowt[0:1, :], K.consts["ones_f_row"][0:1, 0:1])
        K.sem["pe"].inc(inst)
        K.sem["pe"].wait(K.dve)
        inst = K.dve.tensor_copy(cols[0:64, i:i + 1], ps[0:64, 0:1])
        K.sem["dve"].inc(inst)
        pp_done_real(K, ppf, inst)
        K.sem["dve"].wait(K.dve)
        K.sem["dve"].inc(K.dve.tensor_copy(cols[64:128, i:i + 1], cols[0:64, i:i + 1]))
    for nm, cst, sc in (("cq", "c_cq", 0), ("sq", "c_sq", 1), ("ck", "c_ck", 2), ("sk", "c_sk", 3)):
        m = K.sb(f"ffB_{nm}", [128, L], F32)
        maps[nm] = m
        K.load(m[0:64, :], K.inp[cst])
        K.load(m[64:128, :], K.inp[cst])
    K.wait_loads(K.dve)
    K.sem["dve"].wait(K.dve)
    last = None
    for nm, ci in (("cq", 0), ("sq", 1), ("ck", 2), ("sk", 3)):
        last = K.dve.tensor_scalar(maps[nm][:, :], maps[nm][:, :], cols[:, ci:ci + 1],
                                   None, ALU.mult)
    K.sem["dve"].inc(last)
    K.sem["dve"].wait(K.dve)
    qp = K.sb("ffB_qp", [128, L], BF16)
    sq_sb = K.sb("ffB_sq", [128, L], BF16)
    ssrow = K.sb("ffB_ssrow", [1, 2 * L], F32)
    t1 = K.sb("ffB_t1", [128, TT], F32)
    t2 = K.sb("ffB_t2", [128, TT], F32)
    out = K.sb("ffB_out", [128, L], BF16)
    for pp in range(8):
        is_q = pp < 4
        cmap = maps["cq"] if is_q else maps["ck"]
        smap = maps["sq"] if is_q else maps["sk"]
        dst = K.scr["s_qr"] if is_q else K.scr["s_kr"]
        drow = (pp % 4) * 128
        K.sem["pe"].wait(K.sp)
        K.sem["act"].wait(K.sp)
        K.sem["dve"].wait(K.sp)
        K.wait_stores(K.sp)
        K.load(qp[:, :], K.scr["s_qkv"][pp * 128:(pp + 1) * 128, :])
        K.wait_loads(K.act)
        K.sem["act"].inc(K.act.activation(sq_sb[:, :], qp[:, :], AF.Square))
        K.sem["act"].wait(K.pe)
        for hh in range(2):
            for t in range(NT):
                ps = ppf.take()
                inst = K.pe.matmul(ps[0:1, :], K.consts["ones_bf_col"][hh * 64:(hh + 1) * 64, 0:1],
                                   sq_sb[hh * 64:(hh + 1) * 64, t * TT:(t + 1) * TT],
                                   start=True, stop=True)
                K.sem["pe"].inc(inst)
                K.sem["pe"].wait(K.dve)
                inst = K.dve.tensor_copy(ssrow[0:1, hh * L + t * TT:hh * L + (t + 1) * TT],
                                         ps[0:1, :])
                K.sem["dve"].inc(inst)
                pp_done_real(K, ppf, inst)
        K.sem["dve"].wait(K.dve)
        K.sem["dve"].inc(K.dve.tensor_scalar(ssrow[:, :], ssrow[:, :], 1e-24, None, ALU.max))
        K.sem["dve"].wait(K.act)
        K.sem["act"].inc(K.act.activation(ssrow[:, :], ssrow[:, :], AF.Ln))
        K.sem["act"].wait(K.act)
        K.sem["act"].inc(K.act.activation(ssrow[:, :], ssrow[:, :], AF.Exp, scale=-0.5))
        K.sem["act"].wait(K.pe)
        K.sem["act"].wait(K.dve)
        for t in range(NT):
            tsl = slice(t * TT, (t + 1) * TT)
            psA = ppf.take()
            inst = K.pe.matmul(psA[:, :], K.consts["r128_bf"][:, :], qp[:, tsl],
                               start=True, stop=True)
            K.sem["pe"].inc(inst)
            psB = ppf.take()
            for hh in range(2):
                row = ssrow[0:1, hh * L + t * TT:hh * L + (t + 1) * TT]
                lhs = (K.consts["eights_f_row"] if is_q else K.consts["ones_f_row"])
                inst = K.pe.matmul(psB[hh * 64:(hh + 1) * 64, :], lhs[0:1, 0:64], row,
                                   start=True, stop=True)
            K.sem["pe"].inc(inst)
            K.sem["pe"].wait(K.dve)
            K.dve.tensor_mul(t1[:, :], cmap[:, tsl], qp[:, tsl])
            K.sem["dve"].inc(K.dve.tensor_mul(t2[:, :], smap[:, tsl], psA[:, :]))
            K.sem["dve"].wait(K.dve)
            K.sem["dve"].inc(K.dve.tensor_add(t1[:, :], t1[:, :], t2[:, :]))
            K.sem["dve"].wait(K.dve)
            inst = K.dve.tensor_mul(out[:, tsl], t1[:, :], psB[:, :])
            K.sem["dve"].inc(inst)
            pp_done_real(K, ppf, inst, k=2)
        K.sem["dve"].wait(K.sp)
        K.store(dst[drow:drow + 128, :], out[:, :], eng=K.sp)


def emit_l0ffn_attn(K, l):
    W = 512
    qp = K.sb("ffC_q", [128, L], BF16)
    kp = K.sb("ffC_k", [128, L], BF16)
    vp = K.sb("ffC_v", [128, L], BF16)
    v_tm = K.sb("ffC_vtm", [128, L], BF16)
    stash = K.sb("ffC_stash", [128, 640], F32)
    P_sb = K.sb("ffC_P", [128, 640], BF16)
    Dt = K.sb("ffC_D", [128, 128], BF16)
    PT = K.sb("ffC_PT", [128, 128], BF16)
    negm = K.sb("ffC_negm", [128, 1], F32)
    rs = K.sb("ffC_rs", [128, 2], F32)
    og = K.sb("ffC_og", [128, L], BF16)
    ppS = K.pp("ffC_ppS", [128, TT], n=2)
    ppT = K.pp("ffC_ppT", [128, 128], n=2, dtype=BF16)
    ppD = K.pp("ffC_ppD", [128, 128], n=2)
    ppO = K.pp("ffC_ppO", [64, 128], n=2)
    identb = K.consts["ident_bf"]
    for pr in range(4):
        K.sem["pe"].wait(K.sp)
        K.sem["act"].wait(K.sp)
        K.sem["dve"].wait(K.sp)
        K.wait_stores(K.sp)
        K.load(qp[:, :], K.scr["s_qr"][pr * 128:(pr + 1) * 128, :])
        K.load(kp[:, :], K.scr["s_kr"][pr * 128:(pr + 1) * 128, :])
        K.load(vp[:, :], K.scr["s_qkv"][1024 + pr * 128:1024 + (pr + 1) * 128, :])
        K.wait_loads(K.pe)
        for c in range(32):
            ps = ppT.take()
            inst = K.pe.transpose(ps[:, :], vp[:, c * 128:(c + 1) * 128], identb[:, :])
            K.sem["pe"].inc(inst)
            K.sem["pe"].wait(K.dve)
            inst = K.dve.tensor_copy(v_tm[:, c * 128:(c + 1) * 128], ps[:, :])
            K.sem["dve"].inc(inst)
            pp_done_real(K, ppT, inst)
        K.sem["dve"].wait(K.pe)
        for hh in range(2):
            hsl = slice(hh * 64, (hh + 1) * 64)
            for w in range(8):
                for r in range(4):
                    q0 = w * W + r * 128
                    wL = W - r * 128 if w > 0 else 0
                    wR = r * 128 + 128
                    Wt = wL + wR
                    kcn = Wt // 128
                    psL = None
                    if w > 0:
                        psL = ppS.take()
                        inst = K.pe.matmul(psL[:, 0:wL], qp[hsl, q0:q0 + 128],
                                           kp[hsl, (w - 1) * W + r * 128:w * W],
                                           start=True, stop=True)
                        K.sem["pe"].inc(inst)
                    psR = ppS.take()
                    inst = K.pe.matmul(psR[:, 0:wR], qp[hsl, q0:q0 + 128],
                                       kp[hsl, w * W:w * W + wR], start=True, stop=True)
                    K.sem["pe"].inc(inst)
                    K.sem["pe"].wait(K.dve)
                    if w > 0:
                        K.dve.tensor_add(stash[:, 0:128], psL[:, 0:128], K.consts["mask_geq"][:, :])
                        if wL > 128:
                            K.dve.tensor_copy(stash[:, 128:wL], psL[:, 128:wL])
                    if wR > 128:
                        K.dve.tensor_copy(stash[:, wL:wL + wR - 128], psR[:, 0:wR - 128])
                    inst = K.dve.tensor_add(stash[:, Wt - 128:Wt], psR[:, wR - 128:wR],
                                            K.consts["mask_leq"][:, :])
                    K.sem["dve"].inc(inst)
                    if w > 0:
                        pp_done_real(K, ppS, inst, k=2)
                    else:
                        pp_done_real(K, ppS, inst, k=1)
                    K.sem["dve"].wait(K.dve)
                    K.sem["dve"].inc(K.dve.reduce_max(negm[:, :], stash[:, 0:Wt], axis=AX.X,
                                                      negate=True))
                    K.sem["dve"].wait(K.act)
                    K.sem["act"].inc(K.act.activation(P_sb[:, 0:Wt], stash[:, 0:Wt], AF.Exp,
                                                      bias=negm[:, :], accum_out=rs[:, 0:1]))
                    K.sem["act"].wait(K.dve)
                    K.sem["dve"].inc(K.dve.reciprocal(rs[:, 1:2], rs[:, 0:1]))
                    K.sem["dve"].wait(K.dve)
                    K.sem["dve"].inc(K.dve.tensor_scalar(Dt[:, :], identb[:, :], rs[:, 1:2],
                                                         None, ALU.mult))
                    K.sem["dve"].wait(K.pe)
                    psO = ppO.take()
                    lastAV = None
                    for kc in range(kcn):
                        gc = (w - 1) * 4 + r + kc if (w > 0 and kc < wL // 128) else \
                             w * 4 + (kc - (wL // 128))
                        psT = ppD.take()
                        inst = K.pe.matmul(psT[:, :], P_sb[:, kc * 128:(kc + 1) * 128], Dt[:, :],
                                           start=True, stop=True)
                        K.sem["pe"].inc(inst)
                        K.sem["pe"].wait(K.dve)
                        inst = K.dve.tensor_copy(PT[:, :], psT[:, :])
                        K.sem["dve"].inc(inst)
                        pp_done_real(K, ppD, inst)
                        K.sem["dve"].wait(K.pe)
                        lastAV = K.pe.matmul(psO[0:64, :], v_tm[:, gc * 128 + hh * 64:gc * 128 + hh * 64 + 64],
                                             PT[:, :], start=(kc == 0), stop=(kc == kcn - 1))
                    K.sem["pe"].inc(lastAV)
                    K.sem["pe"].wait(K.dve)
                    inst = K.dve.tensor_copy(og[hsl, q0:q0 + 128], psO[0:64, :])
                    K.sem["dve"].inc(inst)
                    pp_done_real(K, ppO, inst)
        K.sem["dve"].wait(K.sp)
        K.store(K.scr["s_og"][pr * 128:(pr + 1) * 128, :], og[:, :], eng=K.sp)


def emit_l0ffn_ln(K, l):
    src = K.scr["s_y"]
    gb = K.sb("ffE_gb", [128, 16], F32)
    rowgb = K.sb("ffE_rowgb", [8, 128], F32)
    ysb = K.sb("ffE_y", [128, DC * TT], F32)
    bfsb = K.sb("ffE_bf", [128, DC * TT], BF16)
    sqsb = K.sb("ffE_sq", [128, DC * TT], BF16)
    xg = K.sb("ffE_xg", [128, DC * TT], BF16)
    rows = K.sb("ffE_rows", [1, 6 * TT], F32)
    pp_stat = K.pp("ffE_ps", [1, TT], n=4)
    pp_bc = K.pp("ffE_pb", [128, TT], n=4)
    onesb = K.consts["ones_bf_col"]
    ones_f = K.consts["ones_f_row"]
    for i, nm in enumerate(("lng", "lnb")):
        K.sem["pe"].wait(K.sp)
        K.load(rowgb[:, :], K.inp[f"w_ff{l}_{nm}"])
        K.wait_loads(K.pe)
        K.sem["dve"].wait(K.pe)
        ps = pp_bc.take()
        inst = K.pe.transpose(ps[:, 0:8], rowgb[0:8, :], K.consts["ident_f"][0:8, 0:8])
        K.sem["pe"].inc(inst)
        K.sem["pe"].wait(K.dve)
        inst = K.dve.tensor_copy(gb[:, i * 8:(i + 1) * 8], ps[:, 0:8])
        K.sem["dve"].inc(inst)
        pp_done_real(K, pp_bc, inst)
    if DEV_DBGPAIR:
        K.sem["dve"].wait(K.sp)
        K.store(K.scr["s_x"][0:128, 0:16], gb[:, :], eng=K.sp)
        K.store(K.scr["s_x"][0:8, 32:160], rowgb[:, :], eng=K.sp)
    for t in range(NT):
        t0 = t * TT
        K.sem["dve"].wait(K.sp)
        K.sem["pe"].wait(K.sp)
        K.wait_stores(K.sp)
        half = DC // 2
        K.load(ysb.ap().rearrange("p (c t) -> p c t", c=DC)[:, 0:half, :],
               K.tile_ap(src, t0, TT)[:, 0:half, :])
        if t == 0:
            K.sem["pe"].wait(K.dve)
            K.sem["act"].wait(K.dve)
            for c in range(half, DC):
                K.sem["dve"].inc(K.dve.memset(ysb[:, c * TT:c * TT + 1], 0.0))
                K.load(ysb[:, c * TT + 1:(c + 1) * TT], src[c * 128:(c + 1) * 128, 0:TT - 1])
        else:
            K.load(ysb.ap().rearrange("p (c t) -> p c t", c=DC)[:, half:DC, :],
                   K.tile_ap(src, t0 - 1, TT)[:, half:DC, :])
        K.wait_loads(K.dve)
        K.wait_loads(K.act)
        K.wait_loads(K.pe)
        K.sem["pe"].wait(K.dve)
        K.sem["pe"].wait(K.act)
        K.sem["dve"].wait(K.act)
        K.sem["dve"].inc(K.dve.tensor_copy(bfsb[:, :], ysb[:, :]))
        K.sem["act"].inc(K.act.activation(sqsb[:, :], ysb[:, :], AF.Square))
        K.sem["dve"].wait(K.pe)
        K.sem["act"].wait(K.pe)
        ps_sum = pp_stat.take()
        for c in range(DC):
            K.pe.matmul(ps_sum[:, :], onesb[:, :], bfsb[:, c * TT:(c + 1) * TT],
                        start=(c == 0), stop=(c == DC - 1))
        ps_sq = pp_stat.take()
        last = None
        for c in range(DC):
            last = K.pe.matmul(ps_sq[:, :], onesb[:, :], sqsb[:, c * TT:(c + 1) * TT],
                               start=(c == 0), stop=(c == DC - 1))
        K.sem["pe"].inc(last)
        K.sem["pe"].wait(K.dve)
        mean = rows[0:1, 0:TT]
        var = rows[0:1, TT:2 * TT]
        rstd = rows[0:1, 2 * TT:3 * TT]
        negmean = rows[0:1, 3 * TT:4 * TT]
        m2 = rows[0:1, 4 * TT:5 * TT]
        lnv = rows[0:1, 5 * TT:6 * TT]
        pp_done_real(K, pp_stat, K.dve.tensor_scalar(mean, ps_sum[:, :], 1.0 / D, None, ALU.mult))
        inst = K.dve.tensor_scalar(var, ps_sq[:, :], 1.0 / D, None, ALU.mult)
        K.sem["dve"].inc(inst)
        pp_done_real(K, pp_stat, inst)
        K.sem["dve"].wait(K.dve)
        K.sem["dve"].inc(K.dve.tensor_mul(m2, mean, mean))
        K.sem["dve"].wait(K.dve)
        K.sem["dve"].inc(K.dve.tensor_sub(var, var, m2))
        K.sem["dve"].wait(K.dve)
        K.sem["dve"].inc(K.dve.tensor_scalar(negmean, mean, -1.0, None, ALU.mult))
        K.sem["dve"].wait(K.act)
        K.sem["act"].inc(K.act.activation(lnv, var, AF.Ln, bias=1e-5))
        K.sem["act"].wait(K.act)
        K.sem["act"].inc(K.act.activation(rstd, lnv, AF.Exp, scale=-0.5))
        K.sem["act"].wait(K.pe)
        negmB = pp_bc.take()
        K.pe.matmul(negmB[:, :], ones_f[:, :], negmean, start=True, stop=True)
        rstdB = pp_bc.take()
        K.sem["pe"].inc(K.pe.matmul(rstdB[:, :], ones_f[:, :], rstd, start=True, stop=True))
        K.sem["pe"].wait(K.dve)
        last = None
        for c in range(DC):
            cs = slice(c * TT, (c + 1) * TT)
            K.dve.tensor_add(ysb[:, cs], ysb[:, cs], negmB[:, :])
            K.dve.tensor_mul(ysb[:, cs], ysb[:, cs], rstdB[:, :])
            last = K.dve.tensor_scalar(xg[:, cs], ysb[:, cs], gb[:, c:c + 1],
                                       gb[:, 8 + c:8 + c + 1], ALU.mult, ALU.add)
        K.sem["dve"].inc(last)
        K.bump("dve", pp_bc.sem, 2)
        pp_bc.dones += 2
        K.sem["dve"].wait(K.sp)
        K.store(K.tile_ap(K.scr["s_xg"], t0, TT),
                xg.ap().rearrange("p (c t) -> p c t", c=DC), eng=K.sp)


def emit_l0ffn_glu(K, l):
    HPC = 22
    w_dram = K.inp[f"w_ff{l}_w1"]
    N = 5632
    wsb = K.sb("ffF_w", [128, DC * N], BF16)
    for kc in range(DC):
        K.load(wsb[:, kc * N:(kc + 1) * N], w_dram[kc * 128:(kc + 1) * 128, :], cast=True)
    brow = K.sb("ffF_bias", [1, N], BF16)
    K.load(brow[:, :], K.inp[f"w_ff{l}_b1"], cast=True)
    xsb = K.sb("ffF_x", [128, DC * TT], BF16)
    osb = K.sb("ffF_o", [128, HPC * TT], BF16)
    tmp2 = K.sb("ffF_t2", [128, TT], F32)
    pp = K.pp("ffF_pp", [128, TT], n=4)
    onesr = K.consts["ones_bf_row"]
    for t in range(NT):
        t0 = t * TT
        K.sem["pe"].wait(K.sp)
        K.wait_stores(K.sp)
        K.load(xsb.ap().rearrange("p (c t) -> p c t", c=DC),
               K.tile_ap(K.scr["s_xg"], t0, TT))
        K.wait_loads(K.pe)
        for j in range(HPC):
            psA = pp.take()
            K.pe.matmul(psA[:, :], brow[0:1, j * 128:(j + 1) * 128], onesr[:, :],
                        start=True, stop=False)
            for kc in range(DC):
                K.pe.matmul(psA[:, :], wsb[:, kc * N + j * 128:kc * N + (j + 1) * 128],
                            xsb[:, kc * TT:(kc + 1) * TT], start=False, stop=(kc == DC - 1))
            psB = pp.take()
            K.pe.matmul(psB[:, :], brow[0:1, 2816 + j * 128:2816 + (j + 1) * 128], onesr[:, :],
                        start=True, stop=False)
            last = None
            for kc in range(DC):
                last = K.pe.matmul(psB[:, :], wsb[:, kc * N + 2816 + j * 128:kc * N + 2816 + (j + 1) * 128],
                                   xsb[:, kc * TT:(kc + 1) * TT], start=False, stop=(kc == DC - 1))
            K.sem["pe"].inc(last)
            K.sem["pe"].wait(K.act)
            K.sem["act"].inc(K.act.activation(tmp2[:, :], psB[:, :], AF.Gelu))
            K.sem["act"].wait(K.dve)
            inst = K.dve.tensor_mul(osb[:, j * TT:(j + 1) * TT], psA[:, :], tmp2[:, :])
            K.sem["dve"].inc(inst)
            pp_done_real(K, pp, inst, k=2)
        K.sem["dve"].wait(K.sp)
        K.store(K.tile_ap(K.scr["s_glu"], t0, TT, chunks=HPC),
                osb.ap().rearrange("p (c t) -> p c t", c=HPC), eng=K.sp)




def _l1_slopes():
    return [2.0 ** (-0.5 * (i + 1)) for i in range(16)]


def emit_l1attn(K):
    old_es = K.phase_es
    with ExitStack() as es:
        K.phase_es = es
        osb_holder = {}

        def evac(ncx, nw, t, ps, osb, pp):
            osb_holder["osb"] = osb
            cs = slice(ncx * TT, (ncx + 1) * TT)
            if ncx < 8:
                inst = K.dve.tensor_scalar(osb[:, cs], ps[:, :], 0.125, None, ALU.mult)
            else:
                inst = K.dve.tensor_copy(osb[:, cs], ps[:, :])
            K.sem["dve"].inc(inst)
            pp_done_real(K, pp, inst)

        def store_cb(t, osb):
            t0 = t * TT
            K.sem["dve"].wait(K.sp)
            for i, dst in enumerate(("s_q", "s_k", "s_v")):
                K.store(K.tile_ap(K.scr[dst], t0, TT),
                        osb[:, i * DC * TT:(i + 1) * DC * TT].rearrange("p (c t) -> p c t", c=DC),
                        eng=K.sp)

        emit_proj(K, "l1A", K.inp["w_l1a_wqkv"], D, 3072, K.scr["s_xnb"],
                  bias_row_dram=K.inp["w_l1a_bqkv"], evac=evac, store_cb=store_cb)
        K.phase_es = old_es
    K.barrier()
    with ExitStack() as es:
        K.phase_es = es
        emit_l1attn_core(K)
        K.phase_es = old_es
    K.barrier()
    with ExitStack() as es:
        K.phase_es = es
        emit_proj(K, "l1C", K.inp["w_l1a_wout"], D, D, K.scr["s_o"],
                  out_dram=K.scr["s_a"], out_dtype=F32,
                  bias_row_dram=K.inp["w_l1a_bout"])
        K.phase_es = old_es


def emit_l1attn_core(K):
    slopes = _l1_slopes()
    posB = K.sb("l1B_posB", [128, L], F32)
    q_h = K.sb("l1B_qh", [64, L], BF16)
    k_h = K.sb("l1B_kh", [64, L], BF16)
    vp = K.sb("l1B_v", [128, L], BF16)
    v_tm = K.sb("l1B_vtm", [128, L], BF16)
    stash = K.sb("l1B_stash", [128, L], F32)
    P_sb = K.sb("l1B_P", [128, L], BF16)
    Dt = K.sb("l1B_D", [128, 128], BF16)
    PT = K.sb("l1B_PT", [128, 128], BF16)
    negm = K.sb("l1B_negm", [128, 1], F32)
    rs = K.sb("l1B_rs", [128, 2], F32)
    o_sb = K.sb("l1B_o", [128, L], BF16)
    ppS = K.pp("l1B_ppS", [128, TT], n=2)
    ppT = K.pp("l1B_ppT", [128, 128], n=2, dtype=BF16)
    ppD = K.pp("l1B_ppD", [128, 128], n=2)
    ppO = K.pp("l1B_ppO", [64, 128], n=2)
    identb = K.consts["ident_bf"]
    ppf = ppS
    for t in range(NT):
        ps = ppf.take()
        inst = K.pe.matmul(ps[:, :], K.consts["ones_f_row"][:, :],
                           K.consts["pos"][0:1, t * TT:(t + 1) * TT], start=True, stop=True)
        K.sem["pe"].inc(inst)
        K.sem["pe"].wait(K.dve)
        inst = K.dve.tensor_copy(posB[:, t * TT:(t + 1) * TT], ps[:, :])
        K.sem["dve"].inc(inst)
        pp_done_real(K, ppf, inst)
    for pr in range(8):
        K.sem["pe"].wait(K.sp)
        K.sem["dve"].wait(K.sp)
        K.sem["act"].wait(K.sp)
        K.wait_stores(K.sp)
        K.load(vp[:, :], K.scr["s_v"][pr * 128:(pr + 1) * 128, :])
        K.wait_loads(K.pe)
        for c in range(32):
            ps = ppT.take()
            inst = K.pe.transpose(ps[:, :], vp[:, c * 128:(c + 1) * 128], identb[:, :])
            K.sem["pe"].inc(inst)
            K.sem["pe"].wait(K.dve)
            inst = K.dve.tensor_copy(v_tm[:, c * 128:(c + 1) * 128], ps[:, :])
            K.sem["dve"].inc(inst)
            pp_done_real(K, ppT, inst)
        for hh in range(2):
            h = 2 * pr + hh
            slope = slopes[h]
            K.sem["pe"].wait(K.sp)
            K.load(q_h[:, :], K.scr["s_q"][h * 64:(h + 1) * 64, :])
            K.load(k_h[:, :], K.scr["s_k"][h * 64:(h + 1) * 64, :])
            K.wait_loads(K.pe)
            for qt in range(32):
                nkt = qt // 4 + 1
                Wt = 128 * qt + 128
                for kt in range(nkt):
                    w = 512 if kt < nkt - 1 else (qt % 4) * 128 + 128
                    c0 = kt * 512
                    psS = ppS.take()
                    inst = K.pe.matmul(psS[:, 0:w], q_h[:, qt * 128:(qt + 1) * 128],
                                       k_h[:, c0:c0 + w], start=True, stop=True)
                    K.sem["pe"].inc(inst)
                    K.sem["pe"].wait(K.dve)
                    inst = K.dve.scalar_tensor_tensor(stash[:, c0:c0 + w], posB[:, c0:c0 + w],
                                                      slope, psS[:, 0:w], ALU.mult, ALU.add)
                    K.sem["dve"].inc(inst)
                    pp_done_real(K, ppS, inst)
                K.sem["dve"].wait(K.dve)
                K.sem["dve"].inc(K.dve.tensor_add(stash[:, Wt - 128:Wt], stash[:, Wt - 128:Wt],
                                                  K.consts["mask_leq"][:, :]))
                K.sem["dve"].wait(K.dve)
                K.sem["dve"].inc(K.dve.reduce_max(negm[:, :], stash[:, 0:Wt], axis=AX.X,
                                                  negate=True))
                K.sem["dve"].wait(K.act)
                K.sem["act"].inc(K.act.activation(P_sb[:, 0:Wt], stash[:, 0:Wt], AF.Exp,
                                                  bias=negm[:, :], accum_out=rs[:, 0:1]))
                K.sem["act"].wait(K.dve)
                K.sem["dve"].inc(K.dve.reciprocal(rs[:, 1:2], rs[:, 0:1]))
                K.sem["dve"].wait(K.dve)
                K.sem["dve"].inc(K.dve.tensor_scalar(Dt[:, :], identb[:, :], rs[:, 1:2],
                                                     None, ALU.mult))
                K.sem["dve"].wait(K.pe)
                psO = ppO.take()
                lastAV = None
                for c in range(qt + 1):
                    psD = ppD.take()
                    inst = K.pe.matmul(psD[:, :], P_sb[:, c * 128:(c + 1) * 128], Dt[:, :],
                                       start=True, stop=True)
                    K.sem["pe"].inc(inst)
                    K.sem["pe"].wait(K.dve)
                    inst = K.dve.tensor_copy(PT[:, :], psD[:, :])
                    K.sem["dve"].inc(inst)
                    pp_done_real(K, ppD, inst)
                    K.sem["dve"].wait(K.pe)
                    lastAV = K.pe.matmul(psO[0:64, :],
                                         v_tm[:, c * 128 + hh * 64:c * 128 + hh * 64 + 64],
                                         PT[:, :], start=(c == 0), stop=(c == qt))
                K.sem["pe"].inc(lastAV)
                K.sem["pe"].wait(K.dve)
                inst = K.dve.tensor_copy(o_sb[hh * 64:(hh + 1) * 64, qt * 128:(qt + 1) * 128],
                                         psO[0:64, :])
                K.sem["dve"].inc(inst)
                pp_done_real(K, ppO, inst)
        K.sem["dve"].wait(K.sp)
        K.store(K.scr["s_o"][pr * 128:(pr + 1) * 128, :], o_sb[:, :], eng=K.sp)




def emit_moe(K):
    old_es = K.phase_es
    comb_tm = K.sb("moe_comb", [128, 32 * 8], F32)
    with ExitStack() as es:
        K.phase_es = es
        emit_moe_router(K, comb_tm)
        K.phase_es = old_es
    if DEV_DBGPAIR:
        K.sem["dve"].wait(K.sp)
        K.store(K.scr["s_y"][0:128, 0:256], comb_tm[:, :], eng=K.sp)
    K.barrier()
    for p in range(DEV_MOE_PASSES):
        e = p - 1
        with ExitStack() as es:
            K.phase_es = es
            if p == 0:
                w1 = K.inp["w_moe_sw1"]
                b1 = K.inp["w_moe_sb1"]
            else:
                w1 = K.inp["w_moe_ew1"][e]
                b1 = K.inp["w_moe_eb1"][e:e + 1, :]

            def gelu_evac(ncx, nw, t, ps, osb, pp):
                K.sem["pe"].wait(K.act)
                cs = slice(ncx * TT, (ncx + 1) * TT)
                inst = K.act.activation(osb[:, cs], ps[:, :], AF.Gelu_apprx_tanh)
                K.sem["act"].inc(inst)
                pp_done_real(K, pp, inst, eng="act")

            emit_proj(K, f"moeA{p}", w1, D, 4096, K.scr["s_xnb"],
                      bias_row_dram=b1, evac=gelu_evac, out_dram=K.scr["s_g"])
            K.phase_es = old_es
        K.barrier()
        with ExitStack() as es:
            K.phase_es = es
            emit_moe_b(K, p, comb_tm)
            K.phase_es = old_es
        K.barrier()


def emit_moe_router(K, comb_tm):
    wr = K.sb("moeR_w", [128, 64], F32)
    br = K.sb("moeR_br", [1, 8], F32)
    xf = K.sb("moeR_x", [128, DC * 128], F32)
    r0 = K.sb("moeR_r", [128, 8 * 6], F32)
    cvec = K.sb("moeR_c", [128, 4], F32)
    pp = K.pp("moeR_pp", [128, TT], n=2)
    for kc in range(DC):
        K.load(wr[:, kc * 8:(kc + 1) * 8], K.inp["w_moe_rw"][kc * 128:(kc + 1) * 128, :])
    K.load(br[:, :], K.inp["w_moe_rb"])
    for c in range(32):
        t0 = c * 128
        K.sem["pe"].wait(K.sp)
        K.load(xf.ap().rearrange("p (k t) -> p k t", k=DC),
               K.tile_ap(K.scr["s_xn"], t0, 128, chunks=DC))
        K.wait_loads(K.pe)
        ps = pp.take()
        K.pe.matmul(ps[:, 0:8], K.consts["ones_f_row"][:, :], br[:, :], start=True, stop=False)
        last = None
        for kc in range(DC):
            last = K.pe.matmul(ps[:, 0:8], xf[:, kc * 128:(kc + 1) * 128], wr[:, kc * 8:(kc + 1) * 8],
                               start=False, stop=(kc == DC - 1))
        K.sem["pe"].inc(last)
        K.sem["pe"].wait(K.dve)
        K.sem["dve"].inc(K.dve.tensor_copy(r0[:, 0:8], ps[:, 0:8]))
        K.sem["dve"].wait(K.dve)
        inst = K.dve.reduce_max(cvec[:, 0:1], r0[:, 0:8], axis=AX.X, negate=True)
        K.sem["dve"].inc(inst)
        pp_done_real(K, pp, inst)
        K.sem["dve"].wait(K.act)
        K.sem["act"].inc(K.act.activation(r0[:, 8:16], r0[:, 0:8], AF.Exp, bias=cvec[:, 0:1],
                                          accum_out=cvec[:, 1:2]))
        K.sem["act"].wait(K.dve)
        K.sem["dve"].inc(K.dve.reciprocal(cvec[:, 2:3], cvec[:, 1:2]))
        K.sem["dve"].wait(K.dve)
        K.sem["dve"].inc(K.dve.tensor_scalar(r0[:, 0:8], r0[:, 8:16], cvec[:, 2:3], None, ALU.mult))
        K.sem["dve"].wait(K.dve)
        K.sem["dve"].inc(K.dve.reduce_max(cvec[:, 0:1], r0[:, 0:8], axis=AX.X))
        K.sem["dve"].wait(K.dve)
        K.sem["dve"].inc(K.dve.tensor_scalar(r0[:, 24:32], r0[:, 0:8], cvec[:, 0:1], None, ALU.is_ge))
        K.sem["dve"].wait(K.dve)
        K.dve.tensor_mul(r0[:, 32:40], r0[:, 24:32], r0[:, 0:8])
        K.sem["dve"].inc(K.dve.tensor_sub(r0[:, 40:48], r0[:, 0:8], r0[:, 24:32]))
        K.sem["dve"].wait(K.dve)
        K.sem["dve"].inc(K.dve.tensor_sub(r0[:, 40:48], r0[:, 40:48], r0[:, 32:40]))
        K.sem["dve"].wait(K.dve)
        K.sem["dve"].inc(K.dve.reduce_max(cvec[:, 3:4], r0[:, 40:48], axis=AX.X))
        K.sem["dve"].wait(K.dve)
        K.sem["dve"].inc(K.dve.tensor_scalar(r0[:, 24:32], r0[:, 0:8], cvec[:, 3:4], None, ALU.is_ge))
        K.sem["dve"].wait(K.dve)
        K.sem["dve"].inc(K.dve.tensor_mul(r0[:, 16:24], r0[:, 0:8], r0[:, 24:32]))
        K.sem["dve"].wait(K.dve)
        K.sem["dve"].inc(K.dve.tensor_reduce(cvec[:, 1:2], r0[:, 16:24], axis=AX.X, op=ALU.add))
        K.sem["dve"].wait(K.dve)
        K.sem["dve"].inc(K.dve.reciprocal(cvec[:, 2:3], cvec[:, 1:2]))
        K.sem["dve"].wait(K.dve)
        K.sem["dve"].inc(K.dve.tensor_scalar(comb_tm[:, c * 8:(c + 1) * 8], r0[:, 16:24],
                                             cvec[:, 2:3], None, ALU.mult))
        K.sem["dve"].wait(K.dve)


def emit_moe_b(K, p, comb_tm):
    e = p - 1
    if p == 0:
        w2_dram = K.inp["w_moe_sw2"]
        b2 = K.inp["w_moe_sb2"]
    else:
        w2_dram = K.inp["w_moe_ew2"][e]
        b2 = K.inp["w_moe_eb2"][e:e + 1, :]
    KC = 32
    wsb = K.sb("moeB_w", [128, KC * D], BF16)
    for kc in range(KC):
        K.load(wsb[:, kc * D:(kc + 1) * D], w2_dram[kc * 128:(kc + 1) * 128, :], cast=True)
    brow = K.sb("moeB_bias", [1, D], BF16)
    K.load(brow[:, :], b2, cast=True)
    xsb = K.sb("moeB_x", [128, KC * TT], BF16)
    ysb = K.sb("moeB_y", [128, DC * TT], F32)
    tmp = K.sb("moeB_tmp", [128, TT], F32)
    combB = None
    if p > 0:
        combB = K.sb("moeB_cB", [128, L], F32)
        ppc = K.pp("moeB_ppc", [128, 128], n=2)
        De = K.sb("moeB_De", [128, 128], F32)
        for c in range(32):
            K.sem["pe"].wait(K.dve)
            inst = K.dve.tensor_scalar(De[:, :], K.consts["ident_f"][:, :],
                                       comb_tm[:, c * 8 + e:c * 8 + e + 1], None, ALU.mult)
            K.sem["dve"].inc(inst)
            K.sem["dve"].wait(K.pe)
            ps = ppc.take()
            inst = K.pe.matmul(ps[:, :], K.consts["ones128_f"][:, :], De[:, :],
                               start=True, stop=True)
            K.sem["pe"].inc(inst)
            K.sem["pe"].wait(K.dve)
            inst = K.dve.tensor_copy(combB[:, c * 128:(c + 1) * 128], ps[:, :])
            K.sem["dve"].inc(inst)
            pp_done_real(K, ppc, inst)
    pp = K.pp("moeB_pp", [128, TT], n=4)
    onesr = K.consts["ones_bf_row"]
    for t in range(NT):
        t0 = t * TT
        K.sem["pe"].wait(K.sp)
        K.sem["dve"].wait(K.sp)
        K.wait_stores(K.sp)
        K.load(xsb.ap().rearrange("p (c t) -> p c t", c=KC),
               K.tile_ap(K.scr["s_g"], t0, TT, chunks=KC))
        K.load(ysb.ap().rearrange("p (c t) -> p c t", c=DC),
               K.tile_ap(K.scr["s_a"], t0, TT, chunks=DC))
        K.wait_loads(K.pe)
        K.wait_loads(K.dve)
        for oc in range(DC):
            ps = pp.take()
            K.pe.matmul(ps[:, :], brow[0:1, oc * 128:(oc + 1) * 128], onesr[:, :],
                        start=True, stop=False)
            last = None
            for kc in range(KC):
                last = K.pe.matmul(ps[:, :], wsb[:, kc * D + oc * 128:kc * D + (oc + 1) * 128],
                                   xsb[:, kc * TT:(kc + 1) * TT], start=False, stop=(kc == KC - 1))
            K.sem["pe"].inc(last)
            K.sem["pe"].wait(K.dve)
            cs = slice(oc * TT, (oc + 1) * TT)
            if p == 0:
                inst = K.dve.tensor_copy(ysb[:, cs], ps[:, :])
                K.sem["dve"].inc(inst)
                pp_done_real(K, pp, inst)
            else:
                inst = K.dve.tensor_mul(tmp[:, :], ps[:, :], combB[:, t0:t0 + TT])
                K.sem["dve"].inc(inst)
                pp_done_real(K, pp, inst)
                K.sem["dve"].wait(K.dve)
                K.sem["dve"].inc(K.dve.tensor_add(ysb[:, cs], ysb[:, cs], tmp[:, :]))
        K.sem["dve"].wait(K.sp)
        K.store(K.tile_ap(K.scr["s_a"], t0, TT, chunks=DC),
                ysb.ap().rearrange("p (c t) -> p c t", c=DC), eng=K.sp)


def emit_p0(K):
    emit_sandwich_ln(K, K.inp["x_fm"], None, None, K.scr["s_xnb"], pass_b_only=True)


def emit_ln1(K):
    emit_sandwich_ln(K, K.scr["s_a"], K.inp["x_fm"], K.scr["s_x"], None,
                     xn_f32_dram=K.scr["s_xn"])


def emit_ln2(K):
    emit_sandwich_ln(K, K.scr["s_a"], K.scr["s_x"], K.scr["s_x2"], K.scr["s_xnb"])


def emit_ln3(K):
    emit_sandwich_ln(K, K.scr["s_a"], K.scr["s_x2"], K.scr["s_x"], K.scr["s_xnb"],
                     xn_f32_dram=K.scr["s_xn"])


def emit_ln4(K):
    emit_sandwich_ln(K, K.scr["s_a"], K.scr["s_x"], K.out["out_fm"], None)


PHASES = [
    ("p0", emit_p0),
    ("l0attn", emit_l0attn),
    ("ln1", emit_ln1),
    ("ff0", lambda K: emit_l0ffn_layer(K, 0)),
    ("ff1", lambda K: emit_l0ffn_layer(K, 1)),
    ("ln2", emit_ln2),
    ("l1attn", emit_l1attn),
    ("ln3", emit_ln3),
    ("moe", emit_moe),
    ("ln4", emit_ln4),
]


def build_program(upto=None, dump=()):
    nc = bass.Bass("TRN2", target_bir_lowering=False)
    es = ExitStack()
    K = Kctx(nc, es)
    declare_io(nc, K)
    with es:
        init_consts(K)
        K.barrier()
        phases = PHASES if upto is None else PHASES[:upto]
        for name, emit in phases:
            with ExitStack() as pes:
                K.phase_es = pes
                emit(K)
                K.phase_es = es
            K.barrier()
        for name in dump:
            src_ = K.scr[name]
            dbg = nc.dram_tensor("dbg_" + name, list(src_.tensor.shape),
                                 src_.tensor.dtype, kind="ExternalOutput").ap()
            K.store(dbg, src_, eng=K.sp)
        K.barrier()
    return nc



def kernel(**inputs):
    shared = prep_shared(inputs)
    x = np.asarray(inputs["L0_attn_x"], dtype=np.float32)
    nc = build_program(upto=DEV_UPTO, dump=DEV_DUMP)
    in_maps = []
    for b in range(8):
        m = dict(shared)
        m["x_fm"] = np.ascontiguousarray(x[b].T)
        in_maps.append(m)
    res = run_bass_kernel_spmd(nc, in_maps, core_ids=list(range(8)))
    out = np.empty((8, L, D), np.float32)
    for b in range(8):
        out[b] = res.results[b]["out_fm"].T
    if DEV_DUMP:
        kernel.dev_results = res.results
    return out
